# version 10
# baseline (speedup 1.0000x reference)
import math
import numpy as np
from contextlib import ExitStack

import concourse.bass as bass
import concourse.mybir as mybir
from concourse.bass_utils import run_bass_kernel_spmd

F32 = mybir.dt.float32
BF16 = mybir.dt.bfloat16
U16 = mybir.dt.uint16
AF = mybir.ActivationFunctionType
ALU = mybir.AluOpType
AX = mybir.AxisListType

N_CORES = 8


class Ct:
    def __init__(self, sem):
        self.sem = sem
        self.n = 0


class Emit:
    ENGINES = ("sy", "sc", "ve", "te", "gp")

    def __init__(self):
        self.prog = {k: [] for k in self.ENGINES}
        self._lastwait = {}
        self.selfct = {}

    def _auto(self, ek):
        ct = self.selfct.get(ek)
        if ct is not None and ct.n > 0:
            self.wait(ek, (ct, ct.n))

    def op(self, ek, f):
        self._auto(ek)
        self.prog[ek].append(f)

    def opi(self, ek, ct, f, k=1):
        self._auto(ek)
        ct.n += k
        sem = ct.sem
        def g(e, f=f, sem=sem, k=k):
            f(e).then_inc(sem, k)
        self.prog[ek].append(g)
        return (ct, ct.n)

    def wait(self, ek, tok):
        if tok is None:
            return
        ct, val = tok
        if val <= 0:
            return
        key = (ek, id(ct))
        if self._lastwait.get(key, 0) >= val:
            return
        self._lastwait[key] = val
        sem = ct.sem
        self.prog[ek].append(lambda e, sem=sem, val=val: e.wait_ge(sem, val))

    def replay(self, ek, eng):
        for f in self.prog[ek]:
            f(eng)


def build_program(D=4096, I=14336, RC=2048, CH=512, stop_after=None):
    KF = D // 2 + 1
    NK = (KF + 127) // 128
    KP = NK * 128
    NL = D // 128
    NCH = RC // CH
    NRT = CH // 128
    NI = I // 128
    TC = 512
    NHC = D // TC
    RG = min(512, RC)
    NRG = RC // RG
    NGRT = RG // 128
    NTILE = RC // 128
    ATC = min(TC, RC)
    NRC = RC // ATC
    SC_INV = 2.0 * math.pi / D

    nc = bass.Bass("TRN2", target_bir_lowering=False)

    x_in = nc.dram_tensor("x_in", [RC, D], F32, kind="ExternalInput").ap()
    gateT = nc.dram_tensor("gateT", [I // 128, 128, D // 128, 128], F32,
                           kind="ExternalInput").ap()
    upT = nc.dram_tensor("upT", [I // 128, 128, D // 128, 128], F32,
                         kind="ExternalInput").ap()
    downT = nc.dram_tensor("downT", [I, D], F32, kind="ExternalInput").ap()
    lnw_in = nc.dram_tensor("lnw_in", [1, D], F32, kind="ExternalInput").ap()
    midx_f = nc.dram_tensor("midx_f", [2, KP // 128, 128, D // 128, 128], U16, kind="ExternalInput").ap()
    midx_i = nc.dram_tensor("midx_i", [2, KP, D], U16, kind="ExternalInput").ap()
    alpha_in = nc.dram_tensor("alpha_in", [2, 128, NK], F32, kind="ExternalInput").ap()
    ident_in = nc.dram_tensor("ident_in", [128, 128], F32, kind="ExternalInput").ap()
    out_d = nc.dram_tensor("out_d", [RC, D], F32, kind="ExternalOutput").ap()

    tabF = nc.dram_tensor("tabF", [2, NK, 128, NL, 128], BF16, kind="Internal").ap()
    tabI6 = nc.dram_tensor("tabI6", [NHC, 128, NK, TC], BF16, kind="Internal").ap()
    tabI2 = nc.dram_tensor("tabI2", [NHC, 128, NK, 2, TC], BF16, kind="Internal").ap()
    fscr = nc.dram_tensor("fscr", [2, 128, NK, CH], BF16, kind="Internal").ap()
    x1scr = nc.dram_tensor("x1scr", [RC, D], F32, kind="Internal").ap()
    hscr = nc.dram_tensor("hscr", [NRG, 128, NI, RG], BF16, kind="Internal").ap()
    dscr = nc.dram_tensor("dscr", [NI, 128, D], BF16, kind="Internal").ap()

    with ExitStack() as ctx:
        E = ctx.enter_context

        SB_BASE = 16512
        SB_TOP = nc.SBUF_PARTITION_SIZE_BYTES - 32

        def at(name, shape, dtype, off):
            nbytes = int(np.prod(shape[1:])) * mybir.dt.size(dtype)
            assert SB_BASE + off + nbytes <= SB_TOP, (name, off, nbytes)
            return nc.alloc_sbuf_tensor_at(name, shape, dtype, offset=SB_BASE + off)

        K = 1024
        ident = at("ident", [128, 128], F32, 0)
        alpha = at("alpha", [128, 2, NK], F32, 512)
        sv = at("sv", [128, 64], F32, 672)
        stats = at("stats", [128, NTILE, 2], F32, 928)
        statsB = at("statsB", [128, NTILE, 2], F32, 1056)
        ones_row = at("ones_row", [1, 128], F32, 1184)
        PH = 1728
        BUDGET = SB_TOP - SB_BASE - PH

        ustage = at("ustage", [128, 2, max(KP, D)], U16, PH + 0)
        t32 = at("t32", [128, max(KP, D)], F32, PH + 16 * K)
        tf32 = at("tf32", [128, max(KP, D)], F32, PH + 32 * K)
        tb16 = at("tb16", [128, 2, max(KP, D)], BF16, PH + 48 * K)

        xstage = at("xstage", [128, D], F32, PH + 0)
        s32 = at("s32", [128, D], F32, PH + 0)
        slotAB = at("slotAB", [128, NL, CH], BF16, PH + 16 * K)
        fslot = at("fslot", [128, 2, NK, CH], BF16, PH + 48 * K)
        OFF_P = 48 * K + 2 * NK * CH * 2
        Pslot = at("Pslot", [128, NK, CH], BF16, PH + OFF_P)
        OFF_CORR = OFF_P + NK * CH * 2
        corrslot = at("corrslot", [128, NRT, D], BF16, PH + OFF_CORR)
        OFF_TMP = OFF_CORR + NRT * D * 2
        e16 = at("e16", [128, D], BF16, PH + OFF_TMP)
        mask16 = at("mask16", [128, D], BF16, PH + OFF_TMP + D * 2)
        ystage = at("ystage", [128, 4, TC], F32, PH + OFF_TMP)
        xlc = at("xlc", [128, NRT, TC], F32, PH + OFF_TMP + 8 * K)
        OFF_TB = OFF_TMP + 16 * K
        zstage = at("zstage", [128, 2, TC], F32, PH + OFF_TB)
        sq16 = at("sq16", [128, TC], BF16, PH + OFF_TB + 4 * K)
        OFF_TABF = OFF_TB + 5 * K + 256
        tabFbuf = at("tabFbuf", [128, 2, NL, 128], BF16, PH + OFF_TABF)
        OFF_TABI = OFF_TABF + 2 * NL * 128 * 2
        tabIbuf = at("tabIbuf", [128, 2, NK, TC], BF16, PH + OFF_TABI)
        tabI2buf = at("tabI2buf", [128, NK, 2, TC], BF16, PH + OFF_TABI)
        NKA = min(NK, (NL * CH * 2) // (2 * TC * 2))
        NKB = NK - NKA
        tabI2bA = at("tabI2bA", [128, NKA, 2, TC], BF16, PH + 16 * K)
        tabI2bB = (at("tabI2bB", [128, NKB, 2, TC], BF16, PH + OFF_P)
                   if NKB > 0 else None)
        OFF_FRL = OFF_TABI + 2 * NK * TC * 2
        freload = at("freload", [128, 2, CH], BF16, PH + OFF_FRL)
        M_END = OFF_FRL + 2 * CH * 2
        assert M_END <= BUDGET, ("M overlay", M_END, BUDGET)

        nx16 = at("nx16", [128, NL, RC], BF16, PH + 0)
        OFF_N = NL * RC * 2
        x1stage = at("x1stage", [128, D], F32, PH + OFF_N)
        nxstage = at("nxstage", [128, D], F32, PH + OFF_N + 16 * K)
        lnwb = at("lnwb", [128, D], F32, PH + OFF_N + 32 * K)
        lnw_row = at("lnw_row", [1, D], F32, PH + OFF_N + 48 * K)
        nsq16 = at("nsq16", [128, D], BF16, PH + OFF_N + 64 * K)
        assert OFF_N + 72 * K <= BUDGET

        gbuf = at("gbuf", [128, 2, NL, 128], BF16, PH + OFF_N)
        ubuf = at("ubuf", [128, 2, NL, 128], BF16, PH + OFF_N + 16 * K)
        hstage = at("hstage", [128, 2, NRC, ATC], BF16, PH + OFF_N + 32 * K)
        dstage = at("dstage", [128, 2, D], BF16, PH + OFF_N + 40 * K)
        asig = at("asig", [128, 2, TC], F32, PH + OFF_N + 56 * K)
        ah1 = at("ah1", [128, 2, TC], F32, PH + OFF_N + 60 * K)
        assert OFF_N + 64 * K <= BUDGET

        HQW_ = D // max(1, D // 1024)
        hslab = at("hslab", [128, NI, RG], BF16, PH + 0)
        OFF_B = NI * RG * 2
        dbufq = at("dbufq", [128, 6, HQW_], BF16, PH + OFF_B)
        x1q = at("x1q", [128, 4, HQW_], F32, PH + OFF_B + 12 * K)
        zq = at("zq", [128, 2, HQW_], F32, PH + OFF_B + 28 * K)
        sq16q = at("sq16q", [128, HQW_], BF16, PH + OFF_B + 36 * K)
        bz = at("bz", [128, D], F32, PH + OFF_B + 38 * K + 256)
        assert OFF_B + 55 * K <= BUDGET

        pall = E(nc.psum_tensor("pall", [128, 8, TC], F32))

        def mkct(name):
            return Ct(E(nc.semaphore(name)))

        cInit = mkct("cInit")
        cLnw = mkct("cLnw")
        cUst = [mkct("cUst0"), mkct("cUst1")]
        cTbst = [mkct("cTbst0"), mkct("cTbst1")]
        cX = mkct("cX")
        cTabF = [mkct("cTabF0"), mkct("cTabF1")]
        cFsp = [mkct("cFsp0"), mkct("cFsp1")]
        cTabI = [mkct("cTabI0"), mkct("cTabI1")]
        cTabI2 = [mkct("cTabI2a"), mkct("cTabI2b")]
        cXlc = [mkct("cXlc%d" % i) for i in range(NRT)]
        cZst = [mkct("cZst0"), mkct("cZst1")]
        cZq = [mkct("cZq0"), mkct("cZq1")]
        cFrl = mkct("cFrl")
        cX1ld = mkct("cX1ld")
        cHsto = [mkct("cHsto0"), mkct("cHsto1")]
        cDscr = [mkct("cDscr0"), mkct("cDscr1")]
        cHsl = mkct("cHsl")
        cDbuf = [mkct("cDbuf%d" % i) for i in range(6)]
        cBx1 = [mkct("cBx1%d" % i) for i in range(4)]
        cOut = mkct("cOut")
        cGb = [mkct("cGb0"), mkct("cGb1")]
        cUb = [mkct("cUb0"), mkct("cUb1")]
        cDstg = [mkct("cDstg0"), mkct("cDstg1")]
        cTE = mkct("cTE")
        cVE = mkct("cVE")
        cSC = mkct("cSC")
        cGP = mkct("cGP")

        ALLCT = ([cInit, cLnw, cX, cFrl, cX1ld, cHsl, cOut,
                  cTE, cVE, cSC, cGP] + cUst + cTbst + cTabF + cFsp + cTabI + cTabI2 +
                 cXlc + cZst + cZq + cHsto + cDbuf + cBx1 + cGb + cUb + cDstg + cDscr)

        em = Emit()
        em.selfct = {"ve": cVE, "sc": cSC, "gp": cGP}
        op, opi = em.op, em.opi

        def barrier():
            for ek in Emit.ENGINES:
                for ct in ALLCT:
                    em.wait(ek, (ct, ct.n))

        class _Stop(Exception):
            pass

        def ckpt(name):
            if stop_after == name:
                raise _Stop()

        def waitl(ek, toks):
            for t in toks:
                em.wait(ek, t)

        def col(i):
            return sv[:, i:i + 1]

        def _emit_all():
            t_id = opi("sy", cInit, lambda e: e.dma_start(out=ident[:], in_=ident_in), 16)
            t_al = opi("sy", cInit, lambda e: e.dma_start(
                out=alpha[:], in_=alpha_in.rearrange("a p k -> p a k")), 16)
            t_init = (cInit, cInit.n)
            t_cpi = opi("ve", cVE, lambda e: e.memset(col(0), -math.pi))
            t_eps = opi("ve", cVE, lambda e: e.memset(col(1), 1e-5))
            t_or = opi("ve", cVE, lambda e: e.memset(ones_row[:], 1.0))

            tok_ust = [None, None]
            tok_tb = [None, None]
            tok_t32 = None
            tok_tf32 = None
            for tab in range(2):
                for kt in range(NK):
                    par = kt % 2
                    waitl("sy", [tok_ust[par]])
                    tld = opi("sy", cUst[par], lambda e, tab=tab, kt=kt, par=par:
                              e.dma_start(
                        out=ustage[:, par, 0:NL * 128].rearrange(
                            "p (l j) -> p l j", j=128),
                        in_=midx_f[tab, kt]), 16)
                    waitl("ve", [tld, tok_t32])
                    tcv = opi("ve", cVE, lambda e, par=par: e.tensor_copy(
                        t32[:, 0:NL * 128], ustage[:, par, 0:NL * 128]))
                    tok_ust[par] = tcv
                    waitl("sc", [tcv, t_cpi, tok_tb[par]])
                    tsn = opi("sc", cSC, lambda e, par=par: e.activation(
                        tb16[:, par, 0:NL * 128], t32[:, 0:NL * 128], AF.Sin,
                        bias=col(0), scale=SC_INV))
                    tok_t32 = tsn
                    waitl("sy", [tsn])
                    tst = opi("sy", cTbst[par], lambda e, tab=tab, kt=kt, par=par:
                              e.dma_start(
                        out=tabF[tab, kt].rearrange("p l j -> p l j"),
                        in_=tb16[:, par, 0:NL * 128].rearrange(
                            "p (l j) -> p l j", j=128)), 16)
                    tok_tb[par] = tst
            for tab in range(2):
                for kt in range(NK):
                    par = kt % 2
                    waitl("sy", [tok_ust[par]])
                    tld = opi("sy", cUst[par], lambda e, tab=tab, kt=kt, par=par: e.dma_start(
                        out=ustage[:, par, 0:D],
                        in_=midx_i[tab, kt * 128:(kt + 1) * 128, :]), 16)
                    waitl("ve", [tld, tok_t32])
                    tcv = opi("ve", cVE, lambda e, par=par: e.tensor_copy(
                        t32[:, 0:D], ustage[:, par, 0:D]))
                    tok_ust[par] = tcv
                    waitl("sc", [tcv, tok_tf32])
                    tsn = opi("sc", cSC, lambda e: e.activation(
                        tf32[:, 0:D], t32[:, 0:D], AF.Sin, bias=col(0), scale=SC_INV))
                    tok_t32 = tsn
                    waitl("ve", [tsn, t_al, tok_tb[par]])
                    tmu = opi("ve", cVE, lambda e, tab=tab, kt=kt, par=par: e.tensor_scalar(
                        tb16[:, par, 0:D], tf32[:, 0:D], alpha[:, tab, kt:kt + 1], None,
                        op0=ALU.mult))
                    tok_tf32 = tmu
                    waitl("sy", [tmu])
                    if tab == 0:
                        opi("sy", cTbst[par], lambda e, kt=kt, par=par: e.dma_start(
                            out=tabI6[:, :, kt].rearrange("c p j -> p c j"),
                            in_=tb16[:, par, 0:D].rearrange("p (c j) -> p c j", j=TC)), 16)
                    t2 = opi("sy", cTbst[par], lambda e, tab=tab, kt=kt, par=par: e.dma_start(
                        out=tabI2[:, :, kt, tab].rearrange("c p j -> p c j"),
                        in_=tb16[:, par, 0:D].rearrange("p (c j) -> p c j", j=TC)), 16)
                    tok_tb[par] = t2
            tok_T0 = (cTbst[0], cTbst[0].n)
            tok_T1 = (cTbst[1], cTbst[1].n)
            ckpt("T")


            barrier()
            tok_xstage = None
            tok_tabFb = [tok_T0, tok_T1]
            tok_tabIb = [tok_T0, tok_T1]
            tok_tabI2b = [tok_T1, None]
            tok_fsp = [None, None]
            tok_zst = [None, None]
            tok_s32 = None
            tok_slotAB_w = None
            tok_sT_last = None
            tok_sq16 = None
            tok_xlc = [None] * NRT
            tok_frl = None
            tok_cbank = [None] * NRT

            for c in range(NCH):
                rb = c * CH
                tok_chunk_sc = (cSC, cSC.n)
                tok_chunk_gp = (cGP, cGP.n)
                tok_slot_readers = (cTE, cTE.n)

                tok_tb01 = [None, None]
                for rt in range(NRT):
                    waitl("gp", [tok_xstage])
                    txd = opi("gp", cX, lambda e, rb=rb, rt=rt: e.dma_start(
                        out=xstage[:], in_=x_in[rb + rt * 128: rb + (rt + 1) * 128, :]), 16)
                    for lt in range(NL):
                        bk = lt % 2
                        waitl("te", [txd, tok_tb01[bk], tok_slot_readers, t_init])
                        ttp = opi("te", cTE, lambda e, lt=lt, bk=bk: e.transpose(
                            pall[:, bk, 0:128], xstage[:, lt * 128:(lt + 1) * 128],
                            ident[:]))
                        waitl("ve", [ttp, tok_slotAB_w])
                        tev = opi("ve", cVE, lambda e, lt=lt, rt=rt, bk=bk: e.tensor_copy(
                            slotAB[:, lt, rt * 128:(rt + 1) * 128], pall[:, bk, 0:128]))
                        tok_tb01[bk] = tev
                    tok_xstage = ttp
                tok_xT = (cVE, cVE.n)
                ckpt('M3.%d' % c)

                tok_fbank = [None, None]
                for half in range(2):
                    for kt in range(NK):
                        par = (half * NK + kt) % 2
                        waitl("sc", [tok_tabFb[par]])
                        tld = opi("sc", cTabF[par], lambda e, half=half, kt=kt, par=par:
                                  e.dma_start(out=tabFbuf[:, par],
                                              in_=tabF[half, kt]), 16)
                        bk = 2 + (kt % 2)
                        waitl("te", [tld, tok_xT, tok_fbank[kt % 2]])
                        for lt in range(NL):
                            if lt < NL - 1:
                                op("te", lambda e, par=par, lt=lt, bk=bk: e.matmul(
                                    pall[:, bk, 0:CH], tabFbuf[:, par, lt, :],
                                    slotAB[:, lt, :], start=(lt == 0), stop=False))
                            else:
                                tmm = opi("te", cTE, lambda e, par=par, lt=lt, bk=bk:
                                          e.matmul(pall[:, bk, 0:CH],
                                                   tabFbuf[:, par, lt, :],
                                                   slotAB[:, lt, :],
                                                   start=False, stop=True))
                        tok_tabFb[par] = tmm
                        waitl("ve", [tmm])
                        tev = opi("ve", cVE, lambda e, half=half, kt=kt, bk=bk:
                                  e.tensor_copy(fslot[:, half, kt, :], pall[:, bk, 0:CH]))
                        tok_fbank[kt % 2] = tev
                tok_f = (cVE, cVE.n)
                ckpt('M4.%d' % c)

                waitl("sy", [tok_f, tok_frl])
                for half in range(2):
                    tok_fsp[half] = opi("sy", cFsp[half], lambda e, half=half: e.dma_start(
                        out=fscr[half], in_=fslot[:, half]), 16)

                tok_ptmp = [None] * 4
                for kt in range(NK):
                    sl0, sl1 = (kt % 2) * 2, (kt % 2) * 2 + 1
                    waitl("sc", [tok_f, tok_ptmp[sl0], tok_chunk_gp])
                    opi("sc", cSC, lambda e, kt=kt, sl0=sl0: e.activation(
                        ystage[:, sl0, 0:CH], fslot[:, 0, kt, :], AF.Square))
                    ts1 = opi("sc", cSC, lambda e, kt=kt, sl1=sl1: e.activation(
                        ystage[:, sl1, 0:CH], fslot[:, 1, kt, :], AF.Square))
                    waitl("ve", [ts1, tok_slot_readers])
                    tad = opi("ve", cVE, lambda e, kt=kt, sl0=sl0, sl1=sl1: e.tensor_add(
                        Pslot[:, kt, :], ystage[:, sl0, 0:CH], ystage[:, sl1, 0:CH]))
                    tok_ptmp[sl0] = tad
                    tok_ptmp[sl1] = tad
                tok_P = (cVE, cVE.n)
                ckpt('M5.%d' % c)

                for tc in range(NHC):
                    par = tc % 2
                    waitl("sy", [tok_tabIb[par]])
                    tld = opi("sy", cTabI[par], lambda e, tc=tc, par=par: e.dma_start(
                        out=tabIbuf[:, par], in_=tabI6[tc]), 16)
                    for rt in range(NRT):
                        bk = 4 + rt
                        waitl("te", [tld, tok_P, tok_cbank[rt]])
                        for kt in range(NK):
                            if kt < NK - 1:
                                op("te", lambda e, kt=kt, rt=rt, par=par, bk=bk: e.matmul(
                                    pall[:, bk, :], Pslot[:, kt, rt * 128:(rt + 1) * 128],
                                    tabIbuf[:, par, kt, :], start=(kt == 0), stop=False))
                            else:
                                tmm = opi("te", cTE,
                                          lambda e, kt=kt, rt=rt, par=par, bk=bk:
                                          e.matmul(pall[:, bk, :],
                                                   Pslot[:, kt, rt * 128:(rt + 1) * 128],
                                                   tabIbuf[:, par, kt, :],
                                                   start=False, stop=True))
                        waitl("ve", [tmm, tok_chunk_sc])
                        tev = opi("ve", cVE, lambda e, rt=rt, tc=tc, bk=bk: e.tensor_copy(
                            corrslot[:, rt, tc * TC:(tc + 1) * TC], pall[:, bk, :]))
                        tok_cbank[rt] = tev
                    tok_tabIb[par] = (cTE, cTE.n)
                tok_corr = (cVE, cVE.n)
                ckpt('M6.%d' % c)

                for rt in range(NRT):
                    mx8 = sv[:, 6:14]
                    e8 = sv[:, 14:22]
                    waitl("ve", [tok_corr, tok_s32, (cSC, cSC.n)])
                    tm8 = opi("ve", cVE, lambda e, rt=rt, mx8=mx8: e.max(
                        out=mx8, in_=corrslot[:, rt, :]))
                    waitl("ve", [tm8])
                    tng = opi("ve", cVE, lambda e, mx8=mx8: e.tensor_scalar(
                        col(3), mx8[:, 0:1], -1.0, None, op0=ALU.mult))
                    waitl("sc", [tng])
                    tex = opi("sc", cSC, lambda e, mx8=mx8, e8=e8: e.activation(
                        e8, mx8, AF.Exp, bias=col(3)))
                    waitl("ve", [tex])
                    tzs = opi("ve", cVE, lambda e, e8=e8: e.reduce_sum(
                        col(5), e8, axis=AX.X))
                    waitl("ve", [tzs])
                    tiz = opi("ve", cVE, lambda e: e.reciprocal(col(4), col(5)))
                    te16 = opi("sc", cSC, lambda e, rt=rt: e.activation(
                        e16[:], corrslot[:, rt, :], AF.Exp, bias=col(3)))
                    tmk = opi("ve", cVE, lambda e, rt=rt, mx8=mx8: e.tensor_scalar(
                        mask16[:], corrslot[:, rt, :], mx8[:, 7:8], None, op0=ALU.is_ge))
                    waitl("ve", [te16, tiz, tmk])
                    ts32 = opi("ve", cVE, lambda e: e.scalar_tensor_tensor(
                        s32[:], e16[:], col(4), mask16[:], op0=ALU.mult, op1=ALU.mult))
                    for lt in range(NL):
                        bk = lt % 2
                        waitl("te", [ts32, tok_tb01[bk]])
                        ttp = opi("te", cTE, lambda e, lt=lt, bk=bk: e.transpose(
                            pall[:, bk, 0:128], s32[:, lt * 128:(lt + 1) * 128],
                            ident[:]))
                        waitl("ve", [ttp])
                        tev = opi("ve", cVE, lambda e, lt=lt, rt=rt, bk=bk: e.tensor_copy(
                            slotAB[:, lt, rt * 128:(rt + 1) * 128], pall[:, bk, 0:128]))
                        tok_tb01[bk] = tev
                    tok_s32 = ttp
                tok_sT = (cVE, cVE.n)
                ckpt('M9.%d' % c)

                tok_fbank = [None, None]
                for half in range(2):
                    for kt in range(NK):
                        par = (half * NK + kt) % 2
                        waitl("sc", [tok_tabFb[par]])
                        tld = opi("sc", cTabF[par], lambda e, half=half, kt=kt, par=par:
                                  e.dma_start(out=tabFbuf[:, par],
                                              in_=tabF[half, kt]), 16)
                        bk = 2 + (kt % 2)
                        waitl("te", [tld, tok_sT, tok_fbank[kt % 2]])
                        for lt in range(NL):
                            if lt < NL - 1:
                                op("te", lambda e, par=par, lt=lt, bk=bk: e.matmul(
                                    pall[:, bk, 0:CH], tabFbuf[:, par, lt, :],
                                    slotAB[:, lt, :], start=(lt == 0), stop=False))
                            else:
                                tmm = opi("te", cTE, lambda e, par=par, lt=lt, bk=bk:
                                          e.matmul(pall[:, bk, 0:CH],
                                                   tabFbuf[:, par, lt, :],
                                                   slotAB[:, lt, :],
                                                   start=False, stop=True))
                        tok_tabFb[par] = tmm
                        waitl("ve", [tmm, tok_fsp[half], (cSC, cSC.n)])
                        tev = opi("ve", cVE, lambda e, half=half, kt=kt, bk=bk:
                                  e.tensor_copy(fslot[:, half, kt, :], pall[:, bk, 0:CH]))
                        tok_fbank[kt % 2] = tev
                tok_S = (cVE, cVE.n)
                ckpt('M10.%d' % c)

                tok_ylast = None
                tok_Ykt = [None] * NK
                for kt in range(NK):
                    waitl("sy", [tok_frl, tok_fsp[0], tok_fsp[1]])
                    tld = opi("sy", cFrl, lambda e, kt=kt: e.dma_start(
                        out=freload[:],
                        in_=fscr[:, :, kt].rearrange("h p r -> p h r")), 16)
                    waitl("ve", [tld, tok_S, tok_ylast])
                    opi("ve", cVE, lambda e, kt=kt: e.tensor_mul(
                        ystage[:, 0, 0:CH], freload[:, 0, :], fslot[:, 0, kt, :]))
                    opi("ve", cVE, lambda e, kt=kt: e.tensor_mul(
                        ystage[:, 1, 0:CH], freload[:, 1, :], fslot[:, 1, kt, :]))
                    opi("ve", cVE, lambda e, kt=kt: e.tensor_mul(
                        ystage[:, 2, 0:CH], freload[:, 0, :], fslot[:, 1, kt, :]))
                    tm4 = opi("ve", cVE, lambda e, kt=kt: e.tensor_mul(
                        ystage[:, 3, 0:CH], freload[:, 1, :], fslot[:, 0, kt, :]))
                    tok_frl = tm4
                    waitl("gp", [tm4])
                    opi("gp", cGP, lambda e, kt=kt: e.tensor_add(
                        fslot[:, 0, kt, :], ystage[:, 0, 0:CH], ystage[:, 1, 0:CH]))
                    tgp = opi("gp", cGP, lambda e, kt=kt: e.tensor_sub(
                        fslot[:, 1, kt, :], ystage[:, 2, 0:CH], ystage[:, 3, 0:CH]))
                    tok_ylast = tgp
                    tok_Ykt[kt] = tgp
                tok_Y = (cGP, cGP.n)
                ckpt('M11.%d' % c)

                for rt in range(NRT):
                    opi("ve", cVE, lambda e, rt=rt: e.memset(sv[:, 24 + rt:25 + rt], 0.0))
                    tzero = opi("ve", cVE,
                                lambda e, rt=rt: e.memset(sv[:, 28 + rt:29 + rt], 0.0))
                for lc in range(NHC):
                    par2 = lc % 2
                    if par2 == 0:
                        waitl("gp", [tok_tabI2b[0]])
                        tld = opi("gp", cTabI2[0], lambda e, lc=lc: e.dma_start(
                            out=tabI2buf[:], in_=tabI2[lc]), 16)
                    else:
                        waitl("gp", [tok_tabI2b[1], tok_sT])
                        tld = opi("gp", cTabI2[1], lambda e, lc=lc: e.dma_start(
                            out=tabI2bA[:], in_=tabI2[lc, :, 0:NKA]), 16)
                        if NKB > 0:
                            tld = opi("gp", cTabI2[1], lambda e, lc=lc: e.dma_start(
                                out=tabI2bB[:], in_=tabI2[lc, :, NKA:]), 16)
                    txl = []
                    for rt in range(NRT):
                        waitl("sy", [tok_xlc[rt]])
                        t = opi("sy", cXlc[rt], lambda e, rb=rb, rt=rt, lc=lc: e.dma_start(
                            out=xlc[:, rt, :],
                            in_=x_in[rb + rt * 128: rb + (rt + 1) * 128,
                                     lc * TC:(lc + 1) * TC]), 16)
                        txl.append(t)
                    for rt in range(NRT):
                        bk = 4 + rt
                        waitl("te", [tld, tok_cbank[rt]])
                        for kt in range(NK):
                            if par2 == 0:
                                rh0 = tabI2buf[:, kt, 0, :]
                                rh1 = tabI2buf[:, kt, 1, :]
                            elif kt < NKA:
                                rh0 = tabI2bA[:, kt, 0, :]
                                rh1 = tabI2bA[:, kt, 1, :]
                            else:
                                rh0 = tabI2bB[:, kt - NKA, 0, :]
                                rh1 = tabI2bB[:, kt - NKA, 1, :]
                            waitl("te", [tok_Ykt[kt]])
                            op("te", lambda e, kt=kt, rt=rt, bk=bk, rh0=rh0: e.matmul(
                                pall[:, bk, :], fslot[:, 0, kt, rt * 128:(rt + 1) * 128],
                                rh0, start=(kt == 0), stop=False))
                            if kt < NK - 1:
                                op("te", lambda e, kt=kt, rt=rt, bk=bk, rh1=rh1: e.matmul(
                                    pall[:, bk, :],
                                    fslot[:, 1, kt, rt * 128:(rt + 1) * 128],
                                    rh1, start=False, stop=False))
                            else:
                                tmm = opi("te", cTE,
                                          lambda e, kt=kt, rt=rt, bk=bk, rh1=rh1:
                                          e.matmul(pall[:, bk, :],
                                                   fslot[:, 1, kt, rt * 128:(rt + 1) * 128],
                                                   rh1, start=False, stop=True))
                        zp = (lc * NRT + rt) % 2
                        waitl("ve", [tmm, txl[rt], tok_zst[zp], tzero])
                        tz = opi("ve", cVE, lambda e, zp=zp, bk=bk, rt=rt: e.tensor_add(
                            zstage[:, zp, :], pall[:, bk, :], xlc[:, rt, :]))
                        tok_cbank[rt] = tz
                        tok_xlc[rt] = tz
                        waitl("ve", [tz])
                        tr1 = opi("ve", cVE, lambda e, zp=zp: e.reduce_sum(
                            col(22), zstage[:, zp, :], axis=AX.X))
                        waitl("ve", [tr1])
                        opi("ve", cVE, lambda e, rt=rt: e.tensor_add(
                            sv[:, 24 + rt:25 + rt], sv[:, 24 + rt:25 + rt], col(22)))
                        waitl("sc", [tz, tok_sq16])
                        tsq = opi("sc", cSC, lambda e, zp=zp: e.activation(
                            sq16[:], zstage[:, zp, :], AF.Square))
                        waitl("ve", [tsq])
                        tr2 = opi("ve", cVE, lambda e: e.reduce_sum(
                            col(23), sq16[:], axis=AX.X))
                        tok_sq16 = tr2
                        waitl("ve", [tr2])
                        opi("ve", cVE, lambda e, rt=rt: e.tensor_add(
                            sv[:, 28 + rt:29 + rt], sv[:, 28 + rt:29 + rt], col(23)))
                        waitl("sy", [tz])
                        tst = opi("sy", cZst[zp], lambda e, rb=rb, rt=rt, lc=lc, zp=zp:
                                  e.dma_start(
                                      out=x1scr[rb + rt * 128: rb + (rt + 1) * 128,
                                                lc * TC:(lc + 1) * TC],
                                      in_=zstage[:, zp, :]), 16)
                        tok_zst[zp] = tst
                    tok_tabI2b[par2] = tmm
                for rt in range(NRT):
                    gi = c * NRT + rt
                    tmn = opi("ve", cVE, lambda e, rt=rt: e.tensor_scalar(
                        col(34), sv[:, 24 + rt:25 + rt], 1.0 / D, None, op0=ALU.mult))
                    t2n = opi("ve", cVE, lambda e, rt=rt: e.tensor_scalar(
                        col(35), sv[:, 28 + rt:29 + rt], 1.0 / D, None, op0=ALU.mult))
                    waitl("ve", [tmn, t2n])
                    tvv = opi("ve", cVE, lambda e: e.tensor_mul(col(36), col(34), col(34)))
                    waitl("ve", [tvv])
                    tvs = opi("ve", cVE, lambda e: e.tensor_sub(col(37), col(35), col(36)))
                    waitl("ve", [tvs])
                    tva = opi("ve", cVE, lambda e: e.tensor_add(col(37), col(37), col(1)))
                    waitl("sc", [tva])
                    tsd = opi("sc", cSC, lambda e: e.activation(col(36), col(37), AF.Sqrt))
                    waitl("ve", [tsd])
                    trs = opi("ve", cVE, lambda e, gi=gi: e.reciprocal(
                        stats[:, gi, 0:1], col(36)))
                    waitl("ve", [trs])
                    tmr = opi("ve", cVE, lambda e, gi=gi: e.tensor_mul(
                        col(36), col(34), stats[:, gi, 0:1]))
                    waitl("ve", [tmr])
                    opi("ve", cVE, lambda e, gi=gi: e.tensor_scalar(
                        stats[:, gi, 1:2], col(36), -1.0, None, op0=ALU.mult))

            ckpt('M')
            tok_stats = (cVE, cVE.n)
            tok_z0 = (cZst[0], cZst[0].n)
            tok_z1 = (cZst[1], cZst[1].n)

            barrier()
            t_lr = opi("sy", cLnw, lambda e: e.dma_start(out=lnw_row[:], in_=lnw_in), 16)
            tok_bbank = [None, None]
            for hcx in range(NHC):
                bb = hcx % 2
                waitl("te", [t_or, t_lr, tok_stats, tok_bbank[bb]])
                tmm = opi("te", cTE, lambda e, hcx=hcx, bb=bb: e.matmul(
                    pall[:, bb, :], ones_row[:, :],
                    lnw_row[:, hcx * TC:(hcx + 1) * TC], start=True, stop=True))
                waitl("ve", [tmm])
                tok_bbank[bb] = opi("ve", cVE, lambda e, hcx=hcx, bb=bb: e.tensor_copy(
                    lnwb[:, hcx * TC:(hcx + 1) * TC], pall[:, bb, :]))
            tok_lnwb = (cVE, cVE.n)
            ckpt('Nb')

            tok_x1 = None
            tok_nxs = None
            tok_tb01 = [tok_lnwb, None]
            for gi in range(NTILE):
                waitl("sy", [tok_x1, tok_z0, tok_z1])
                tld = opi("sy", cX1ld, lambda e, gi=gi: e.dma_start(
                    out=x1stage[:], in_=x1scr[gi * 128:(gi + 1) * 128, :]), 16)
                waitl("ve", [tld, tok_stats, tok_nxs, tok_lnwb])
                tx1 = opi("ve", cVE, lambda e, gi=gi: e.tensor_scalar(
                    nxstage[:], x1stage[:], stats[:, gi, 0:1], stats[:, gi, 1:2],
                    op0=ALU.mult, op1=ALU.add))
                tok_x1 = tx1
                waitl("sc", [tx1])
                tsq = opi("sc", cSC, lambda e: e.activation(nsq16[:], nxstage[:], AF.Square))
                waitl("ve", [tsq])
                tr2 = opi("ve", cVE, lambda e: e.reduce_sum(col(23), nsq16[:], axis=AX.X))
                waitl("ve", [tr2])
                tms = opi("ve", cVE, lambda e: e.tensor_scalar(
                    col(27), col(23), 1.0 / D, col(1), op0=ALU.mult, op1=ALU.add))
                waitl("sc", [tms])
                tsd = opi("sc", cSC, lambda e: e.activation(col(26), col(27), AF.Sqrt))
                waitl("ve", [tsd])
                trs = opi("ve", cVE, lambda e: e.reciprocal(col(25), col(26)))
                waitl("ve", [trs])
                opi("ve", cVE, lambda e: e.tensor_scalar(
                    nxstage[:], nxstage[:], col(25), None, op0=ALU.mult))
                tnl = opi("ve", cVE, lambda e: e.tensor_mul(
                    nxstage[:], nxstage[:], lnwb[:]))
                for lt in range(NL):
                    bk = lt % 2
                    waitl("te", [tnl, tok_tb01[bk]])
                    ttp = opi("te", cTE, lambda e, lt=lt, bk=bk: e.transpose(
                        pall[:, bk, 0:128], nxstage[:, lt * 128:(lt + 1) * 128], ident[:]))
                    waitl("ve", [ttp])
                    tev = opi("ve", cVE, lambda e, lt=lt, gi=gi, bk=bk: e.tensor_copy(
                        nx16[:, lt, gi * 128:(gi + 1) * 128], pall[:, bk, 0:128]))
                    tok_tb01[bk] = tev
                tok_nxs = ttp
                ckpt('N.%d' % gi)
            tok_nx = (cVE, cVE.n)
            ckpt('N')

            barrier()
            tok_gb = [None, None]
            tok_ub = [None, None]
            tok_dsin = [None, None]
            tok_dsout = [None, None]
            tok_hst = [None, None]
            tok_gu_banks = [None] * 4
            tok_sig = [None, None]
            tok_h1 = [None, None]
            for ic in range(NI):
                par = ic % 2
                waitl("gp", [tok_gb[par]])
                tgl = opi("gp", cGb[par], lambda e, ic=ic, par=par: e.dma_start(
                    out=gbuf[:, par], in_=gateT[ic]), 16)
                waitl("gp", [tok_ub[par]])
                tul = opi("gp", cUb[par], lambda e, ic=ic, par=par: e.dma_start(
                    out=ubuf[:, par], in_=upT[ic]), 16)
                waitl("gp", [tok_dsout[par]])
                tdl = opi("gp", cDstg[par], lambda e, ic=ic, par=par: e.dma_start(
                    out=dstage[:, par, :], in_=downT[ic * 128:(ic + 1) * 128, :]), 16)
                waitl("sc", [tdl])
                tds = opi("sc", cDscr[par], lambda e, ic=ic, par=par: e.dma_start(
                    out=dscr[ic], in_=dstage[:, par, :]), 16)
                tok_dsout[par] = tds

                last_h = None
                for rc in range(NRC):
                    bkg = (rc % 2) * 2
                    bku = bkg + 1
                    waitl("te", [tgl, tul, tok_nx, tok_gu_banks[bkg], tok_gu_banks[bku]])
                    for lt in range(NL):
                        op("te", lambda e, lt=lt, par=par, rc=rc, bkg=bkg: e.matmul(
                            pall[:, bkg, 0:ATC], gbuf[:, par, lt, :],
                            nx16[:, lt, rc * ATC:(rc + 1) * ATC],
                            start=(lt == 0), stop=(lt == NL - 1)))
                    for lt in range(NL):
                        if lt < NL - 1:
                            op("te", lambda e, lt=lt, par=par, rc=rc, bku=bku: e.matmul(
                                pall[:, bku, 0:ATC], ubuf[:, par, lt, :],
                                nx16[:, lt, rc * ATC:(rc + 1) * ATC],
                                start=(lt == 0), stop=False))
                        else:
                            tmm = opi("te", cTE,
                                      lambda e, lt=lt, par=par, rc=rc, bku=bku:
                                      e.matmul(pall[:, bku, 0:ATC], ubuf[:, par, lt, :],
                                               nx16[:, lt, rc * ATC:(rc + 1) * ATC],
                                               start=False, stop=True))
                    if rc == NRC - 1:
                        tok_gb[par] = tmm
                        tok_ub[par] = tmm
                    sp = rc % 2
                    waitl("sc", [tmm, tok_sig[sp]])
                    tsg = opi("sc", cSC, lambda e, bkg=bkg, sp=sp: e.activation(
                        asig[:, sp, 0:ATC], pall[:, bkg, 0:ATC], AF.Sigmoid))
                    waitl("ve", [tsg, tok_h1[sp], tok_hst[par] if rc == 0 else None])
                    th1 = opi("ve", cVE, lambda e, bkg=bkg, sp=sp: e.tensor_mul(
                        ah1[:, sp, 0:ATC], asig[:, sp, 0:ATC], pall[:, bkg, 0:ATC]))
                    tok_sig[sp] = th1
                    waitl("ve", [th1])
                    th2 = opi("ve", cVE, lambda e, bku=bku, sp=sp, rc=rc, par=par:
                              e.tensor_mul(hstage[:, par, rc, :], ah1[:, sp, 0:ATC],
                                           pall[:, bku, 0:ATC]))
                    tok_h1[sp] = th2
                    tok_gu_banks[bkg] = th2
                    tok_gu_banks[bku] = th2
                    last_h = th2
                waitl("sy", [last_h])
                tho = opi("sy", cHsto[par], lambda e, ic=ic, par=par: e.dma_start(
                    out=hscr.rearrange("g p i r -> i p g r")[ic],
                    in_=hstage[:, par]), 16)
                tok_hst[par] = tho
            tok_A_h0 = (cHsto[0], cHsto[0].n)
            tok_A_h1 = (cHsto[1], cHsto[1].n)
            tok_A_te = (cTE, cTE.n)
            ckpt('A')

            barrier()
            HQ = max(1, D // 1024)
            HQW = D // HQ
            NH2 = HQW // TC
            tok_hslab = tok_A_te
            tok_dbuf = [None] * 6
            tok_bank8 = [None] * 8
            tok_x1q = [None] * 4
            tok_zq = [None, None]
            tok_sq16q = None
            tok_bz = None
            tok_outd = None
            for rg in range(NRG):
                waitl("sy", [tok_hslab, tok_A_h0, tok_A_h1])
                thl = opi("sy", cHsl, lambda e, rg=rg: e.dma_start(
                    out=hslab[:], in_=hscr[rg]), 16)
                tzro = None
                for q in range(8):
                    tzro = opi("ve", cVE, lambda e, q=q: e.memset(
                        sv[:, 24 + q:25 + q], 0.0))
                for hcq in range(HQ):
                    tok_txq = [None] * NGRT
                    for grt in range(NGRT):
                        gi = rg * NGRT + grt
                        xp = grt % 4
                        waitl("sy", [tok_x1q[xp], tok_z0, tok_z1])
                        tok_txq[grt] = opi("sy", cBx1[xp],
                                           lambda e, gi=gi, hcq=hcq, xp=xp: e.dma_start(
                                               out=x1q[:, xp, :],
                                               in_=x1scr[gi * 128:(gi + 1) * 128,
                                                         hcq * HQW:(hcq + 1) * HQW]), 16)
                    for ic in range(NI):
                        dp = ic % 6
                        qe = ("sy", "sc", "gp")[ic % 3]
                        waitl(qe, [tok_dbuf[dp], tok_dsout[0], tok_dsout[1]])
                        tdl = opi(qe, cDbuf[dp], lambda e, ic=ic, dp=dp, hcq=hcq:
                                  e.dma_start(out=dbufq[:, dp, :],
                                              in_=dscr[ic][:, hcq * HQW:(hcq + 1) * HQW]),
                                  16)
                        waitl("te", [tdl, thl])
                        for grt in range(NGRT):
                            for h2 in range(NH2):
                                bk = grt * NH2 + h2
                                if ic == 0:
                                    waitl("te", [tok_bank8[bk]])
                                if grt == NGRT - 1 and h2 == NH2 - 1:
                                    tmm = opi("te", cTE,
                                              lambda e, ic=ic, grt=grt, h2=h2, bk=bk, dp=dp:
                                              e.matmul(pall[:, bk, :],
                                                       hslab[:, ic, grt * 128:(grt + 1) * 128],
                                                       dbufq[:, dp, h2 * TC:(h2 + 1) * TC],
                                                       start=(ic == 0), stop=(ic == NI - 1)))
                                else:
                                    op("te", lambda e, ic=ic, grt=grt, h2=h2, bk=bk, dp=dp:
                                       e.matmul(pall[:, bk, :],
                                                hslab[:, ic, grt * 128:(grt + 1) * 128],
                                                dbufq[:, dp, h2 * TC:(h2 + 1) * TC],
                                                start=(ic == 0), stop=(ic == NI - 1)))
                        tok_dbuf[dp] = tmm
                    for grt in range(NGRT):
                        gi = rg * NGRT + grt
                        xp = grt % 4
                        zp = grt % 2
                        waitl("ve", [tok_txq[grt], tok_stats, tok_zq[zp], tzro])
                        tx1 = opi("ve", cVE, lambda e, gi=gi, xp=xp: e.tensor_scalar(
                            x1q[:, xp, :], x1q[:, xp, :], stats[:, gi, 0:1],
                            stats[:, gi, 1:2], op0=ALU.mult, op1=ALU.add))
                        for h2 in range(NH2):
                            bk = grt * NH2 + h2
                            waitl("ve", [tmm])
                            tz = opi("ve", cVE, lambda e, bk=bk, h2=h2, xp=xp, zp=zp:
                                     e.tensor_add(zq[:, zp, h2 * TC:(h2 + 1) * TC],
                                                  pall[:, bk, :],
                                                  x1q[:, xp, h2 * TC:(h2 + 1) * TC]))
                            tok_bank8[bk] = tz
                        tok_x1q[xp] = tz
                        tr1 = opi("ve", cVE, lambda e, zp=zp: e.reduce_sum(
                            col(22), zq[:, zp, :], axis=AX.X))
                        ta1 = opi("ve", cVE, lambda e, grt=grt: e.tensor_add(
                            sv[:, 24 + grt:25 + grt], sv[:, 24 + grt:25 + grt], col(22)))
                        waitl("sc", [tz, tok_sq16q])
                        tsq = opi("sc", cSC, lambda e, zp=zp: e.activation(
                            sq16q[:], zq[:, zp, :], AF.Square))
                        waitl("ve", [tsq])
                        tr2 = opi("ve", cVE, lambda e: e.reduce_sum(
                            col(23), sq16q[:], axis=AX.X))
                        tok_sq16q = tr2
                        ta2 = opi("ve", cVE, lambda e, grt=grt: e.tensor_add(
                            sv[:, 28 + grt:29 + grt], sv[:, 28 + grt:29 + grt], col(23)))
                        waitl("sy", [ta1, tr2])
                        tzs = opi("sy", cZq[zp], lambda e, gi=gi, hcq=hcq, zp=zp:
                                  e.dma_start(out=x1scr[gi * 128:(gi + 1) * 128,
                                                        hcq * HQW:(hcq + 1) * HQW],
                                              in_=zq[:, zp, :]), 16)
                        tok_zq[zp] = tzs
                tok_z0 = (cZq[0], cZq[0].n)
                tok_z1 = (cZq[1], cZq[1].n)
                for grt in range(NGRT):
                    gi = rg * NGRT + grt
                    tmn = opi("ve", cVE, lambda e, grt=grt: e.tensor_scalar(
                        col(34), sv[:, 24 + grt:25 + grt], 1.0 / D, None, op0=ALU.mult))
                    t2n = opi("ve", cVE, lambda e, grt=grt: e.tensor_scalar(
                        col(35), sv[:, 28 + grt:29 + grt], 1.0 / D, None, op0=ALU.mult))
                    tvv = opi("ve", cVE, lambda e: e.tensor_mul(col(36), col(34), col(34)))
                    tvs = opi("ve", cVE, lambda e: e.tensor_sub(col(37), col(35), col(36)))
                    tva = opi("ve", cVE, lambda e: e.tensor_add(col(37), col(37), col(1)))
                    waitl("sc", [tva])
                    tsd = opi("sc", cSC, lambda e: e.activation(col(36), col(37), AF.Sqrt))
                    waitl("ve", [tsd])
                    trs = opi("ve", cVE, lambda e, gi=gi: e.reciprocal(
                        statsB[:, gi, 0:1], col(36)))
                    tmr = opi("ve", cVE, lambda e, gi=gi: e.tensor_mul(
                        col(36), col(34), statsB[:, gi, 0:1]))
                    opi("ve", cVE, lambda e, gi=gi: e.tensor_scalar(
                        statsB[:, gi, 1:2], col(36), -1.0, None, op0=ALU.mult))
                tok_statsB = (cVE, cVE.n)
                for grt in range(NGRT):
                    gi = rg * NGRT + grt
                    waitl("sy", [tok_bz, tok_z0, tok_z1])
                    tbl = opi("sy", cX1ld, lambda e, gi=gi: e.dma_start(
                        out=bz[:], in_=x1scr[gi * 128:(gi + 1) * 128, :]), 16)
                    waitl("ve", [tbl, tok_statsB, tok_outd])
                    tfin = opi("ve", cVE, lambda e, gi=gi: e.tensor_scalar(
                        bz[:], bz[:], statsB[:, gi, 0:1], statsB[:, gi, 1:2],
                        op0=ALU.mult, op1=ALU.add))
                    waitl("sy", [tfin])
                    tou = opi("sy", cOut, lambda e, gi=gi: e.dma_start(
                        out=out_d[gi * 128:(gi + 1) * 128, :], in_=bz[:]), 16)
                    tok_outd = tou
                    tok_bz = tou
                tok_hslab = (cTE, cTE.n)

            waitl("sy", [(cOut, cOut.n)])
            op("sy", lambda e: e.nop())

        try:
            _emit_all()
        except _Stop:
            for ek in Emit.ENGINES:
                for ct in ALLCT:
                    em.wait(ek, (ct, ct.n))
            tdmy = opi("sy", cOut, lambda e: e.dma_start(
                out=out_d[0:128, 0:128], in_=ident[:]), 16)
            em.wait("sy", tdmy)


        block = E(nc.Block())

        @block.sync
        def _(eng):
            em.replay("sy", eng)

        @block.scalar
        def _(eng):
            em.replay("sc", eng)

        @block.vector
        def _(eng):
            em.replay("ve", eng)

        @block.tensor
        def _(eng):
            em.replay("te", eng)

        @block.gpsimd
        def _(eng):
            em.replay("gp", eng)

    return nc


def make_tables_host(D, KP):
    li = np.arange(D, dtype=np.int64)
    ki = np.arange(KP, dtype=np.int64)
    mf = np.mod(np.outer(li, ki), D)
    mi = np.mod(np.outer(ki, li), D)
    midx_f_flat = np.stack([(mf + 3 * D // 4) % D, (mf + D // 2) % D]).astype(np.uint16)
    NL = D // 128
    NKl = KP // 128
    midx_f = np.ascontiguousarray(
        midx_f_flat.reshape(2, NL, 128, NKl, 128).transpose(0, 3, 2, 1, 4))
    midx_i = np.stack([(mi + 3 * D // 4) % D, (mi + D // 2) % D]).astype(np.uint16)
    alpha = np.where((ki == 0) | (ki == D // 2), 1.0, 2.0)
    alpha[ki > D // 2] = 0.0
    alpha = (alpha / D).astype(np.float64)
    NK = KP // 128
    a_l = alpha.reshape(NK, 128).T
    alpha_in = np.stack([a_l, -a_l]).astype(np.float32)
    return midx_f, midx_i, alpha_in


def kernel(L0_attn_x, L0_ffn_ln_w, L0_ffn_gate_w, L0_ffn_up_w, L0_ffn_down_w):
    x = np.ascontiguousarray(np.asarray(L0_attn_x, dtype=np.float32))
    B, S, D = x.shape
    I = L0_ffn_gate_w.shape[0]
    RC = (B * S) // N_CORES
    SL = S // N_CORES

    NL_, NI_ = D // 128, I // 128
    gateT = np.ascontiguousarray(np.asarray(L0_ffn_gate_w, np.float32).T.reshape(
        NL_, 128, NI_, 128).transpose(2, 1, 0, 3))
    upT = np.ascontiguousarray(np.asarray(L0_ffn_up_w, np.float32).T.reshape(
        NL_, 128, NI_, 128).transpose(2, 1, 0, 3))
    downT = np.ascontiguousarray(np.asarray(L0_ffn_down_w, np.float32).T)
    lnw = np.ascontiguousarray(np.asarray(L0_ffn_ln_w, np.float32).reshape(1, D))
    KP = ((D // 2 + 1 + 127) // 128) * 128
    midx_f, midx_i, alpha_in = make_tables_host(D, KP)
    ident = np.eye(128, dtype=np.float32)

    in_maps = []
    for c in range(N_CORES):
        xs = np.ascontiguousarray(x[:, c * SL:(c + 1) * SL, :].reshape(RC, D))
        in_maps.append(dict(
            x_in=xs, gateT=gateT, upT=upT, downT=downT, lnw_in=lnw,
            midx_f=midx_f, midx_i=midx_i, alpha_in=alpha_in, ident_in=ident))

    nc = build_program(D=D, I=I, RC=RC, CH=512)
    res = run_bass_kernel_spmd(nc, in_maps, core_ids=list(range(N_CORES)))

    out = np.empty((B, S, D), dtype=np.float32)
    for c in range(N_CORES):
        out[:, c * SL:(c + 1) * SL, :] = res.results[c]["out_d"].reshape(B, SL, D)
    return out


# version 12 (final)
# speedup vs baseline: 1.0315x; 1.0315x over previous
import math
import numpy as np
from contextlib import ExitStack

import concourse.bass as bass
import concourse.mybir as mybir
from concourse.bass_utils import run_bass_kernel_spmd

F32 = mybir.dt.float32
BF16 = mybir.dt.bfloat16
U16 = mybir.dt.uint16
AF = mybir.ActivationFunctionType
ALU = mybir.AluOpType
AX = mybir.AxisListType

N_CORES = 8


class Ct:
    def __init__(self, sem):
        self.sem = sem
        self.n = 0


class Emit:
    ENGINES = ("sy", "sc", "ve", "te", "gp")

    def __init__(self):
        self.prog = {k: [] for k in self.ENGINES}
        self._lastwait = {}
        self.selfct = {}

    def _auto(self, ek):
        ct = self.selfct.get(ek)
        if ct is not None and ct.n > 0:
            self.wait(ek, (ct, ct.n))

    def op(self, ek, f):
        self._auto(ek)
        self.prog[ek].append(f)

    def opi(self, ek, ct, f, k=1):
        self._auto(ek)
        ct.n += k
        sem = ct.sem
        def g(e, f=f, sem=sem, k=k):
            f(e).then_inc(sem, k)
        self.prog[ek].append(g)
        return (ct, ct.n)

    def wait(self, ek, tok):
        if tok is None:
            return
        ct, val = tok
        if val <= 0:
            return
        key = (ek, id(ct))
        if self._lastwait.get(key, 0) >= val:
            return
        self._lastwait[key] = val
        sem = ct.sem
        self.prog[ek].append(lambda e, sem=sem, val=val: e.wait_ge(sem, val))

    def replay(self, ek, eng):
        for f in self.prog[ek]:
            f(eng)


def build_program(D=4096, I=14336, RC=2048, CH=512, stop_after=None):
    KF = D // 2 + 1
    NK = (KF + 127) // 128
    KP = NK * 128
    NL = D // 128
    NCH = RC // CH
    NRT = CH // 128
    NI = I // 128
    TC = 512
    NHC = D // TC
    NTC = (KF + TC - 1) // TC
    NTT = NK
    CW = NTC * TC
    RG = min(512, RC)
    NRG = RC // RG
    NGRT = RG // 128
    NTILE = RC // 128
    ATC = min(TC, RC)
    NRC = RC // ATC
    SC_INV = 2.0 * math.pi / D

    nc = bass.Bass("TRN2", target_bir_lowering=False)

    x_in = nc.dram_tensor("x_in", [RC, D], F32, kind="ExternalInput").ap()
    gateT = nc.dram_tensor("gateT", [I // 128, 128, D // 128, 128], F32,
                           kind="ExternalInput").ap()
    upT = nc.dram_tensor("upT", [I // 128, 128, D // 128, 128], F32,
                         kind="ExternalInput").ap()
    downT = nc.dram_tensor("downT", [I, D], F32, kind="ExternalInput").ap()
    lnw_in = nc.dram_tensor("lnw_in", [1, D], F32, kind="ExternalInput").ap()
    midx_f = nc.dram_tensor("midx_f", [2, KP // 128, 128, D // 128, 128], U16, kind="ExternalInput").ap()
    midx_i = nc.dram_tensor("midx_i", [2, KP, D], U16, kind="ExternalInput").ap()
    alpha_in = nc.dram_tensor("alpha_in", [2, 128, NK], F32, kind="ExternalInput").ap()
    ident_in = nc.dram_tensor("ident_in", [128, 128], F32, kind="ExternalInput").ap()
    out_d = nc.dram_tensor("out_d", [RC, D], F32, kind="ExternalOutput").ap()

    tabF = nc.dram_tensor("tabF", [2, NK, 128, NL, 128], BF16, kind="Internal").ap()
    tabI6 = nc.dram_tensor("tabI6", [NTC, 128, NK, TC], BF16, kind="Internal").ap()
    tabFS = nc.dram_tensor("tabFS", [NK, 128, NTT, 128], BF16, kind="Internal").ap()
    tabI2 = nc.dram_tensor("tabI2", [NHC, 128, NK, 2, TC], BF16, kind="Internal").ap()
    fscr = nc.dram_tensor("fscr", [2, 128, NK, CH], BF16, kind="Internal").ap()
    x1scr = nc.dram_tensor("x1scr", [RC, D], F32, kind="Internal").ap()
    hscr = nc.dram_tensor("hscr", [NRG, 128, NI, RG], BF16, kind="Internal").ap()
    dscr = nc.dram_tensor("dscr", [NI, 128, D], BF16, kind="Internal").ap()

    with ExitStack() as ctx:
        E = ctx.enter_context

        SB_BASE = 16512
        SB_TOP = nc.SBUF_PARTITION_SIZE_BYTES - 32

        def at(name, shape, dtype, off):
            nbytes = int(np.prod(shape[1:])) * mybir.dt.size(dtype)
            assert SB_BASE + off + nbytes <= SB_TOP, (name, off, nbytes)
            return nc.alloc_sbuf_tensor_at(name, shape, dtype, offset=SB_BASE + off)

        K = 1024
        ident = at("ident", [128, 128], F32, 0)
        alpha = at("alpha", [128, 2, NK], F32, 512)
        sv = at("sv", [128, 64], F32, 672)
        stats = at("stats", [128, NTILE, 2], F32, 928)
        statsB = at("statsB", [128, NTILE, 2], F32, 1056)
        ones_row = at("ones_row", [1, 128], F32, 1184)
        PH = 1728
        BUDGET = SB_TOP - SB_BASE - PH

        ustage = at("ustage", [128, 2, max(KP, D)], U16, PH + 0)
        t32 = at("t32", [128, max(KP, D)], F32, PH + 16 * K)
        tf32 = at("tf32", [128, max(KP, D)], F32, PH + 32 * K)
        tb16 = at("tb16", [128, 2, max(KP, D)], BF16, PH + 48 * K)
        tbs16 = at("tbs16", [128, NTT * 128], BF16, PH + 64 * K)

        xstage = at("xstage", [128, D], F32, PH + 0)
        s32 = at("s32", [128, D], F32, PH + 0)
        slotAB = at("slotAB", [128, NL, CH], BF16, PH + 16 * K)
        fslot = at("fslot", [128, 2, NK, CH], BF16, PH + 48 * K)
        OFF_P = 48 * K + 2 * NK * CH * 2
        Pslot = at("Pslot", [128, NK, CH], BF16, PH + OFF_P)
        OFF_CORR = OFF_P + NK * CH * 2
        corrslot = at("corrslot", [128, NRT, CW], BF16, PH + OFF_CORR)
        OFF_TMP = OFF_CORR + NRT * D * 2
        e16 = at("e16", [128, D], BF16, PH + OFF_TMP)
        mask16 = at("mask16", [128, D], BF16, PH + OFF_TMP + D * 2)
        ystage = at("ystage", [128, 4, TC], F32, PH + OFF_TMP)
        xlc = at("xlc", [128, NRT, TC], F32, PH + OFF_TMP + 8 * K)
        OFF_TB = OFF_TMP + 16 * K
        zstage = at("zstage", [128, 2, TC], F32, PH + OFF_TB)
        sq16 = at("sq16", [128, TC], BF16, PH + OFF_TB + 4 * K)
        OFF_TABF = OFF_TB + 5 * K + 256
        tabFbuf = at("tabFbuf", [128, 2, NL, 128], BF16, PH + OFF_TABF)
        OFF_TABI = OFF_TABF + 2 * NL * 128 * 2
        tabIbuf = at("tabIbuf", [128, 2, NK, TC], BF16, PH + OFF_TABI)
        tabI2buf = at("tabI2buf", [128, NK, 2, TC], BF16, PH + OFF_TABI)
        NKA = min(NK, (NL * CH * 2) // (2 * TC * 2))
        NKB = NK - NKA
        tabI2bA = at("tabI2bA", [128, NKA, 2, TC], BF16, PH + 16 * K)
        tabI2bB = (at("tabI2bB", [128, NKB, 2, TC], BF16, PH + OFF_P)
                   if NKB > 0 else None)
        OFF_FRL = OFF_TABI + 2 * NK * TC * 2
        freload = at("freload", [128, 2, CH], BF16, PH + OFF_FRL)
        M_END = OFF_FRL + 2 * CH * 2
        assert M_END <= BUDGET, ("M overlay", M_END, BUDGET)

        nx16 = at("nx16", [128, NL, RC], BF16, PH + 0)
        OFF_N = NL * RC * 2
        x1stage = at("x1stage", [128, D], F32, PH + OFF_N)
        nxstage = at("nxstage", [128, D], F32, PH + OFF_N + 16 * K)
        lnwb = at("lnwb", [128, D], F32, PH + OFF_N + 32 * K)
        lnw_row = at("lnw_row", [1, D], F32, PH + OFF_N + 48 * K)
        nsq16 = at("nsq16", [128, D], BF16, PH + OFF_N + 64 * K)
        assert OFF_N + 72 * K <= BUDGET

        gbuf = at("gbuf", [128, 2, NL, 128], BF16, PH + OFF_N)
        ubuf = at("ubuf", [128, 2, NL, 128], BF16, PH + OFF_N + 16 * K)
        hstage = at("hstage", [128, 2, NRC, ATC], BF16, PH + OFF_N + 32 * K)
        dstage = at("dstage", [128, 2, D], BF16, PH + OFF_N + 40 * K)
        asig = at("asig", [128, 2, TC], F32, PH + OFF_N + 56 * K)
        ah1 = at("ah1", [128, 2, TC], F32, PH + OFF_N + 60 * K)
        assert OFF_N + 64 * K <= BUDGET

        HQW_ = D // max(1, D // 1024)
        hslab = at("hslab", [128, NI, RG], BF16, PH + 0)
        OFF_B = NI * RG * 2
        dbufq = at("dbufq", [128, 6, HQW_], BF16, PH + OFF_B)
        x1q = at("x1q", [128, 4, HQW_], F32, PH + OFF_B + 12 * K)
        zq = at("zq", [128, 2, HQW_], F32, PH + OFF_B + 28 * K)
        sq16q = at("sq16q", [128, HQW_], BF16, PH + OFF_B + 36 * K)
        bz = at("bz", [128, D], F32, PH + OFF_B + 38 * K + 256)
        assert OFF_B + 55 * K <= BUDGET

        pall = E(nc.psum_tensor("pall", [128, 8, TC], F32))

        def mkct(name):
            return Ct(E(nc.semaphore(name)))

        cInit = mkct("cInit")
        cLnw = mkct("cLnw")
        cUst = [mkct("cUst0"), mkct("cUst1")]
        cTbst = [mkct("cTbst0"), mkct("cTbst1")]
        cTbsS = mkct("cTbsS")
        cX = mkct("cX")
        cTabF = [mkct("cTabF0"), mkct("cTabF1")]
        cFsp = [mkct("cFsp0"), mkct("cFsp1")]
        cTabI = [mkct("cTabI0"), mkct("cTabI1")]
        cTabI2 = [mkct("cTabI2a"), mkct("cTabI2b")]
        cXlc = [mkct("cXlc%d" % i) for i in range(NRT)]
        cZst = [mkct("cZst0"), mkct("cZst1")]
        cZq = [mkct("cZq0"), mkct("cZq1")]
        cFrl = mkct("cFrl")
        cX1ld = mkct("cX1ld")
        cHsto = [mkct("cHsto0"), mkct("cHsto1")]
        cDscr = [mkct("cDscr0"), mkct("cDscr1")]
        cHsl = mkct("cHsl")
        cDbuf = [mkct("cDbuf%d" % i) for i in range(6)]
        cBx1 = [mkct("cBx1%d" % i) for i in range(4)]
        cOut = mkct("cOut")
        cGb = [mkct("cGb0"), mkct("cGb1")]
        cUb = [mkct("cUb0"), mkct("cUb1")]
        cDstg = [mkct("cDstg0"), mkct("cDstg1")]
        cTE = mkct("cTE")
        cVE = mkct("cVE")
        cSC = mkct("cSC")
        cGP = mkct("cGP")

        ALLCT = ([cInit, cLnw, cX, cFrl, cX1ld, cHsl, cOut,
                  cTE, cVE, cSC, cGP] + cUst + cTbst + [cTbsS] + cTabF + cFsp + cTabI + cTabI2 +
                 cXlc + cZst + cZq + cHsto + cDbuf + cBx1 + cGb + cUb + cDstg + cDscr)

        em = Emit()
        em.selfct = {"ve": cVE, "sc": cSC, "gp": cGP}
        op, opi = em.op, em.opi

        def barrier():
            for ek in Emit.ENGINES:
                for ct in ALLCT:
                    em.wait(ek, (ct, ct.n))

        class _Stop(Exception):
            pass

        def ckpt(name):
            if stop_after == name:
                raise _Stop()

        def waitl(ek, toks):
            for t in toks:
                em.wait(ek, t)

        def col(i):
            return sv[:, i:i + 1]

        def _emit_all():
            t_id = opi("sy", cInit, lambda e: e.dma_start(out=ident[:], in_=ident_in), 16)
            t_al = opi("sy", cInit, lambda e: e.dma_start(
                out=alpha[:], in_=alpha_in.rearrange("a p k -> p a k")), 16)
            t_init = (cInit, cInit.n)
            t_cpi = opi("ve", cVE, lambda e: e.memset(col(0), -math.pi))
            t_eps = opi("ve", cVE, lambda e: e.memset(col(1), 1e-5))
            t_or = opi("ve", cVE, lambda e: e.memset(ones_row[:], 1.0))
            for _i, _w in enumerate([1.0, 2.0, 2.0, 2.0, 1.0, 0.0, 0.0, 0.0]):
                opi("ve", cVE, lambda e, _i=_i, _w=_w: e.memset(
                    sv[:, 48 + _i:49 + _i], _w))

            tok_ust = [None, None]
            tok_tb = [None, None]
            tok_tbs = None
            tok_t32 = None
            tok_tf32 = None
            for tab in range(2):
                for kt in range(NK):
                    par = kt % 2
                    waitl("sy", [tok_ust[par]])
                    tld = opi("sy", cUst[par], lambda e, tab=tab, kt=kt, par=par:
                              e.dma_start(
                        out=ustage[:, par, 0:NL * 128].rearrange(
                            "p (l j) -> p l j", j=128),
                        in_=midx_f[tab, kt]), 16)
                    waitl("ve", [tld, tok_t32])
                    tcv = opi("ve", cVE, lambda e, par=par: e.tensor_copy(
                        t32[:, 0:NL * 128], ustage[:, par, 0:NL * 128]))
                    tok_ust[par] = tcv
                    waitl("sc", [tcv, t_cpi, tok_tb[par]])
                    tsn = opi("sc", cSC, lambda e, par=par: e.activation(
                        tb16[:, par, 0:NL * 128], t32[:, 0:NL * 128], AF.Sin,
                        bias=col(0), scale=SC_INV))
                    tok_t32 = tsn
                    waitl("sy", [tsn])
                    tst = opi("sy", cTbst[par], lambda e, tab=tab, kt=kt, par=par:
                              e.dma_start(
                        out=tabF[tab, kt].rearrange("p l j -> p l j"),
                        in_=tb16[:, par, 0:NL * 128].rearrange(
                            "p (l j) -> p l j", j=128)), 16)
                    tok_tb[par] = tst
                    if tab == 0:
                        waitl("ve", [tsn, tok_tbs])
                        tts = opi("ve", cVE, lambda e, par=par: e.tensor_scalar(
                            tbs16[:], tb16[:, par, 0:NTT * 128], 2.0, None,
                            op0=ALU.mult))
                        opi("ve", cVE, lambda e: e.tensor_scalar(
                            tbs16[0:1, 0:128], tbs16[0:1, 0:128], 0.5, None,
                            op0=ALU.mult))
                        tp2 = opi("ve", cVE, lambda e: e.tensor_scalar(
                            tbs16[0:1, (NTT - 1) * 128:(NTT - 1) * 128 + 128],
                            tbs16[0:1, (NTT - 1) * 128:(NTT - 1) * 128 + 128],
                            0.5, None, op0=ALU.mult))
                        waitl("sy", [tp2])
                        tok_tbs = opi("sy", cTbsS, lambda e, kt=kt: e.dma_start(
                            out=tabFS[kt],
                            in_=tbs16.rearrange("p (t j) -> p t j", j=128)), 16)
            for tab in range(2):
                for kt in range(NK):
                    par = kt % 2
                    waitl("sy", [tok_ust[par]])
                    tld = opi("sy", cUst[par], lambda e, tab=tab, kt=kt, par=par: e.dma_start(
                        out=ustage[:, par, 0:D],
                        in_=midx_i[tab, kt * 128:(kt + 1) * 128, :]), 16)
                    waitl("ve", [tld, tok_t32])
                    tcv = opi("ve", cVE, lambda e, par=par: e.tensor_copy(
                        t32[:, 0:D], ustage[:, par, 0:D]))
                    tok_ust[par] = tcv
                    waitl("sc", [tcv, tok_tf32])
                    tsn = opi("sc", cSC, lambda e: e.activation(
                        tf32[:, 0:D], t32[:, 0:D], AF.Sin, bias=col(0), scale=SC_INV))
                    tok_t32 = tsn
                    waitl("ve", [tsn, t_al, tok_tb[par]])
                    tmu = opi("ve", cVE, lambda e, tab=tab, kt=kt, par=par: e.tensor_scalar(
                        tb16[:, par, 0:D], tf32[:, 0:D], alpha[:, tab, kt:kt + 1], None,
                        op0=ALU.mult))
                    tok_tf32 = tmu
                    waitl("sy", [tmu])
                    if tab == 0:
                        opi("sy", cTbst[par], lambda e, kt=kt, par=par: e.dma_start(
                            out=tabI6[:, :, kt].rearrange("c p j -> p c j"),
                            in_=tb16[:, par, 0:min(CW, D)].rearrange(
                                "p (c j) -> p c j", j=TC)), 16)
                    t2 = opi("sy", cTbst[par], lambda e, tab=tab, kt=kt, par=par: e.dma_start(
                        out=tabI2[:, :, kt, tab].rearrange("c p j -> p c j"),
                        in_=tb16[:, par, 0:D].rearrange("p (c j) -> p c j", j=TC)), 16)
                    tok_tb[par] = t2
            tok_T0 = (cTbst[0], cTbst[0].n)
            tok_T1 = (cTbst[1], cTbst[1].n)
            ckpt("T")


            barrier()
            tok_xstage = None
            tok_tabFb = [tok_T0, tok_T1]
            tok_tabIb = [tok_T0, tok_T1]
            tok_tabI2b = [tok_T1, None]
            tok_fsp = [None, None]
            tok_zst = [None, None]
            tok_s32 = None
            tok_slotAB_w = None
            tok_sT_last = None
            tok_sq16 = None
            tok_xlc = [None] * NRT
            tok_frl = None
            tok_cbank = [None] * NRT

            for c in range(NCH):
                rb = c * CH
                tok_chunk_sc = (cSC, cSC.n)
                tok_chunk_gp = (cGP, cGP.n)
                tok_slot_readers = (cTE, cTE.n)

                tok_tb01 = [None, None]
                for rt in range(NRT):
                    waitl("gp", [tok_xstage])
                    txd = opi("gp", cX, lambda e, rb=rb, rt=rt: e.dma_start(
                        out=xstage[:], in_=x_in[rb + rt * 128: rb + (rt + 1) * 128, :]), 16)
                    for lt in range(NL):
                        bk = lt % 2
                        waitl("te", [txd, tok_tb01[bk], tok_slot_readers, t_init])
                        ttp = opi("te", cTE, lambda e, lt=lt, bk=bk: e.transpose(
                            pall[:, bk, 0:128], xstage[:, lt * 128:(lt + 1) * 128],
                            ident[:]))
                        waitl("ve", [ttp, tok_slotAB_w])
                        tev = opi("ve", cVE, lambda e, lt=lt, rt=rt, bk=bk: e.tensor_copy(
                            slotAB[:, lt, rt * 128:(rt + 1) * 128], pall[:, bk, 0:128]))
                        tok_tb01[bk] = tev
                    tok_xstage = ttp
                tok_xT = (cVE, cVE.n)
                ckpt('M3.%d' % c)

                tok_fbank = [None, None]
                for half in range(2):
                    for kt in range(NK):
                        par = (half * NK + kt) % 2
                        waitl("sc", [tok_tabFb[par]])
                        tld = opi("sc", cTabF[par], lambda e, half=half, kt=kt, par=par:
                                  e.dma_start(out=tabFbuf[:, par],
                                              in_=tabF[half, kt]), 16)
                        bk = 2 + (kt % 2)
                        waitl("te", [tld, tok_xT, tok_fbank[kt % 2]])
                        for lt in range(NL):
                            if lt < NL - 1:
                                op("te", lambda e, par=par, lt=lt, bk=bk: e.matmul(
                                    pall[:, bk, 0:CH], tabFbuf[:, par, lt, :],
                                    slotAB[:, lt, :], start=(lt == 0), stop=False))
                            else:
                                tmm = opi("te", cTE, lambda e, par=par, lt=lt, bk=bk:
                                          e.matmul(pall[:, bk, 0:CH],
                                                   tabFbuf[:, par, lt, :],
                                                   slotAB[:, lt, :],
                                                   start=False, stop=True))
                        tok_tabFb[par] = tmm
                        waitl("ve", [tmm])
                        tev = opi("ve", cVE, lambda e, half=half, kt=kt, bk=bk:
                                  e.tensor_copy(fslot[:, half, kt, :], pall[:, bk, 0:CH]))
                        tok_fbank[kt % 2] = tev
                tok_f = (cVE, cVE.n)
                ckpt('M4.%d' % c)

                waitl("sy", [tok_f, tok_frl])
                for half in range(2):
                    tok_fsp[half] = opi("sy", cFsp[half], lambda e, half=half: e.dma_start(
                        out=fscr[half], in_=fslot[:, half]), 16)

                tok_ptmp = [None] * 4
                for kt in range(NK):
                    sl0, sl1 = (kt % 2) * 2, (kt % 2) * 2 + 1
                    waitl("sc", [tok_f, tok_ptmp[sl0], tok_chunk_gp])
                    opi("sc", cSC, lambda e, kt=kt, sl0=sl0: e.activation(
                        ystage[:, sl0, 0:CH], fslot[:, 0, kt, :], AF.Square))
                    ts1 = opi("sc", cSC, lambda e, kt=kt, sl1=sl1: e.activation(
                        ystage[:, sl1, 0:CH], fslot[:, 1, kt, :], AF.Square))
                    waitl("ve", [ts1, tok_slot_readers])
                    tad = opi("ve", cVE, lambda e, kt=kt, sl0=sl0, sl1=sl1: e.tensor_add(
                        Pslot[:, kt, :], ystage[:, sl0, 0:CH], ystage[:, sl1, 0:CH]))
                    tok_ptmp[sl0] = tad
                    tok_ptmp[sl1] = tad
                tok_P = (cVE, cVE.n)
                ckpt('M5.%d' % c)

                for tc in range(NTC):
                    par = tc % 2
                    waitl("sy", [tok_tabIb[par], tok_tabI2b[0], tok_tabI2b[1],
                                 (cTabI2[0], cTabI2[0].n), (cTabI2[1], cTabI2[1].n)])
                    tld = opi("sy", cTabI[par], lambda e, tc=tc, par=par: e.dma_start(
                        out=tabIbuf[:, par], in_=tabI6[tc]), 16)
                    for rt in range(NRT):
                        bk = 4 + rt
                        waitl("te", [tld, tok_P, tok_cbank[rt]])
                        for kt in range(NK):
                            if kt < NK - 1:
                                op("te", lambda e, kt=kt, rt=rt, par=par, bk=bk: e.matmul(
                                    pall[:, bk, :], Pslot[:, kt, rt * 128:(rt + 1) * 128],
                                    tabIbuf[:, par, kt, :], start=(kt == 0), stop=False))
                            else:
                                tmm = opi("te", cTE,
                                          lambda e, kt=kt, rt=rt, par=par, bk=bk:
                                          e.matmul(pall[:, bk, :],
                                                   Pslot[:, kt, rt * 128:(rt + 1) * 128],
                                                   tabIbuf[:, par, kt, :],
                                                   start=False, stop=True))
                        waitl("ve", [tmm, tok_chunk_sc])
                        tev = opi("ve", cVE, lambda e, rt=rt, tc=tc, bk=bk: e.tensor_copy(
                            corrslot[:, rt, tc * TC:(tc + 1) * TC], pall[:, bk, :]))
                        tok_cbank[rt] = tev
                    tok_tabIb[par] = (cTE, cTE.n)
                if CW > KF:
                    for rt in range(NRT):
                        opi("ve", cVE, lambda e, rt=rt: e.memset(
                            corrslot[:, rt, KF:CW], -3.0e38))
                tok_corr = (cVE, cVE.n)
                ckpt('M6.%d' % c)

                for rt in range(NRT):
                    mx8 = sv[:, 6:14]
                    e8 = sv[:, 14:22]
                    waitl("ve", [tok_corr, tok_s32, (cSC, cSC.n)])
                    tm8 = opi("ve", cVE, lambda e, rt=rt, mx8=mx8: e.max(
                        out=mx8, in_=corrslot[:, rt, 0:CW]))
                    waitl("ve", [tm8])
                    tng = opi("ve", cVE, lambda e, mx8=mx8: e.tensor_scalar(
                        col(3), mx8[:, 0:1], -1.0, None, op0=ALU.mult))
                    waitl("sc", [tng])
                    tex = opi("sc", cSC, lambda e, mx8=mx8, e8=e8: e.activation(
                        e8, mx8, AF.Exp, bias=col(3)))
                    waitl("ve", [tex])
                    tzw = opi("ve", cVE, lambda e, e8=e8: e.tensor_mul(
                        sv[:, 40:48], e8, sv[:, 48:56]))
                    waitl("ve", [tzw])
                    tzs = opi("ve", cVE, lambda e: e.reduce_sum(
                        col(5), sv[:, 40:48], axis=AX.X))
                    waitl("ve", [tzs])
                    tiz = opi("ve", cVE, lambda e: e.reciprocal(col(4), col(5)))
                    te16 = opi("sc", cSC, lambda e, rt=rt: e.activation(
                        e16[:, 0:CW], corrslot[:, rt, 0:CW], AF.Exp, bias=col(3)))
                    tmk = opi("ve", cVE, lambda e, rt=rt, mx8=mx8: e.tensor_scalar(
                        mask16[:, 0:CW], corrslot[:, rt, 0:CW], mx8[:, 4:5], None,
                        op0=ALU.is_ge))
                    waitl("ve", [te16, tiz, tmk])
                    ts32 = opi("ve", cVE, lambda e: e.scalar_tensor_tensor(
                        s32[:, 0:CW], e16[:, 0:CW], col(4), mask16[:, 0:CW],
                        op0=ALU.mult, op1=ALU.mult))
                    for lt in range(NTT):
                        bk = lt % 2
                        waitl("te", [ts32, tok_tb01[bk]])
                        ttp = opi("te", cTE, lambda e, lt=lt, bk=bk: e.transpose(
                            pall[:, bk, 0:128], s32[:, lt * 128:(lt + 1) * 128],
                            ident[:]))
                        waitl("ve", [ttp])
                        tev = opi("ve", cVE, lambda e, lt=lt, rt=rt, bk=bk: e.tensor_copy(
                            slotAB[:, lt, rt * 128:(rt + 1) * 128], pall[:, bk, 0:128]))
                        tok_tb01[bk] = tev
                    tok_s32 = ttp
                tok_sT = (cVE, cVE.n)
                ckpt('M9.%d' % c)

                tok_fbank = [None, None]
                for kt in range(NK):
                    par = kt % 2
                    waitl("sc", [tok_tabFb[par]])
                    tld = opi("sc", cTabF[par], lambda e, kt=kt, par=par:
                              e.dma_start(out=tabFbuf[:, par, 0:NTT, :],
                                          in_=tabFS[kt]), 16)
                    bk = 2 + (kt % 2)
                    waitl("te", [tld, tok_sT, tok_fbank[kt % 2]])
                    for tt in range(NTT):
                        if tt < NTT - 1:
                            op("te", lambda e, par=par, tt=tt, bk=bk: e.matmul(
                                pall[:, bk, 0:CH], tabFbuf[:, par, tt, :],
                                slotAB[:, tt, :], start=(tt == 0), stop=False))
                        else:
                            tmm = opi("te", cTE, lambda e, par=par, tt=tt, bk=bk:
                                      e.matmul(pall[:, bk, 0:CH],
                                               tabFbuf[:, par, tt, :],
                                               slotAB[:, tt, :],
                                               start=False, stop=True))
                    tok_tabFb[par] = tmm
                    waitl("ve", [tmm, tok_fsp[0], tok_fsp[1], (cSC, cSC.n)])
                    tev = opi("ve", cVE, lambda e, kt=kt, bk=bk:
                              e.tensor_copy(fslot[:, 0, kt, :], pall[:, bk, 0:CH]))
                    tok_fbank[kt % 2] = tev
                tok_S = (cVE, cVE.n)

                tok_ylast = None
                tok_Ykt = [None] * NK
                for kt in range(NK):
                    waitl("sy", [tok_frl, tok_fsp[0], tok_fsp[1]])
                    tld = opi("sy", cFrl, lambda e, kt=kt: e.dma_start(
                        out=freload[:],
                        in_=fscr[:, :, kt].rearrange("h p r -> p h r")), 16)
                    waitl("ve", [tld, tok_S, tok_ylast])
                    opi("ve", cVE, lambda e, kt=kt: e.tensor_mul(
                        fslot[:, 1, kt, :], freload[:, 0, :], fslot[:, 0, kt, :]))
                    tgp = opi("ve", cVE, lambda e, kt=kt: e.scalar_tensor_tensor(
                        fslot[:, 0, kt, :], freload[:, 1, :], -1.0,
                        fslot[:, 0, kt, :], op0=ALU.mult, op1=ALU.mult))
                    tok_ylast = tgp
                    tok_frl = tgp
                    tok_Ykt[kt] = tgp
                tok_Y = (cVE, cVE.n)

                for rt in range(NRT):
                    opi("ve", cVE, lambda e, rt=rt: e.memset(sv[:, 24 + rt:25 + rt], 0.0))
                    tzero = opi("ve", cVE,
                                lambda e, rt=rt: e.memset(sv[:, 28 + rt:29 + rt], 0.0))
                for lc in range(NHC):
                    par2 = lc % 2
                    if par2 == 0:
                        waitl("gp", [tok_tabI2b[0], tok_tabIb[0], tok_tabIb[1],
                                     (cTabI[0], cTabI[0].n), (cTabI[1], cTabI[1].n)])
                        tld = opi("gp", cTabI2[0], lambda e, lc=lc: e.dma_start(
                            out=tabI2buf[:], in_=tabI2[lc]), 16)
                    else:
                        waitl("gp", [tok_tabI2b[1], (cTE, cTE.n)])
                        tld = opi("gp", cTabI2[1], lambda e, lc=lc: e.dma_start(
                            out=tabI2bA[:], in_=tabI2[lc, :, 0:NKA]), 16)
                        if NKB > 0:
                            tld = opi("gp", cTabI2[1], lambda e, lc=lc: e.dma_start(
                                out=tabI2bB[:], in_=tabI2[lc, :, NKA:]), 16)
                    txl = []
                    for rt in range(NRT):
                        waitl("sy", [tok_xlc[rt]])
                        t = opi("sy", cXlc[rt], lambda e, rb=rb, rt=rt, lc=lc: e.dma_start(
                            out=xlc[:, rt, :],
                            in_=x_in[rb + rt * 128: rb + (rt + 1) * 128,
                                     lc * TC:(lc + 1) * TC]), 16)
                        txl.append(t)
                    for rt in range(NRT):
                        bk = 4 + rt
                        waitl("te", [tld, tok_cbank[rt]])
                        for kt in range(NK):
                            if par2 == 0:
                                rh0 = tabI2buf[:, kt, 0, :]
                                rh1 = tabI2buf[:, kt, 1, :]
                            elif kt < NKA:
                                rh0 = tabI2bA[:, kt, 0, :]
                                rh1 = tabI2bA[:, kt, 1, :]
                            else:
                                rh0 = tabI2bB[:, kt - NKA, 0, :]
                                rh1 = tabI2bB[:, kt - NKA, 1, :]
                            waitl("te", [tok_Ykt[kt]])
                            op("te", lambda e, kt=kt, rt=rt, bk=bk, rh0=rh0: e.matmul(
                                pall[:, bk, :], fslot[:, 1, kt, rt * 128:(rt + 1) * 128],
                                rh0, start=(kt == 0), stop=False))
                            if kt < NK - 1:
                                op("te", lambda e, kt=kt, rt=rt, bk=bk, rh1=rh1: e.matmul(
                                    pall[:, bk, :],
                                    fslot[:, 0, kt, rt * 128:(rt + 1) * 128],
                                    rh1, start=False, stop=False))
                            else:
                                tmm = opi("te", cTE,
                                          lambda e, kt=kt, rt=rt, bk=bk, rh1=rh1:
                                          e.matmul(pall[:, bk, :],
                                                   fslot[:, 0, kt, rt * 128:(rt + 1) * 128],
                                                   rh1, start=False, stop=True))
                        zp = (lc * NRT + rt) % 2
                        waitl("ve", [tmm, txl[rt], tok_zst[zp], tzero])
                        tz = opi("ve", cVE, lambda e, zp=zp, bk=bk, rt=rt: e.tensor_add(
                            zstage[:, zp, :], pall[:, bk, :], xlc[:, rt, :]))
                        tok_cbank[rt] = tz
                        tok_xlc[rt] = tz
                        waitl("ve", [tz])
                        tr1 = opi("ve", cVE, lambda e, zp=zp: e.reduce_sum(
                            col(22), zstage[:, zp, :], axis=AX.X))
                        waitl("ve", [tr1])
                        opi("ve", cVE, lambda e, rt=rt: e.tensor_add(
                            sv[:, 24 + rt:25 + rt], sv[:, 24 + rt:25 + rt], col(22)))
                        waitl("sc", [tz, tok_sq16])
                        tsq = opi("sc", cSC, lambda e, zp=zp: e.activation(
                            sq16[:], zstage[:, zp, :], AF.Square))
                        waitl("ve", [tsq])
                        tr2 = opi("ve", cVE, lambda e: e.reduce_sum(
                            col(23), sq16[:], axis=AX.X))
                        tok_sq16 = tr2
                        waitl("ve", [tr2])
                        opi("ve", cVE, lambda e, rt=rt: e.tensor_add(
                            sv[:, 28 + rt:29 + rt], sv[:, 28 + rt:29 + rt], col(23)))
                        waitl("sy", [tz])
                        tst = opi("sy", cZst[zp], lambda e, rb=rb, rt=rt, lc=lc, zp=zp:
                                  e.dma_start(
                                      out=x1scr[rb + rt * 128: rb + (rt + 1) * 128,
                                                lc * TC:(lc + 1) * TC],
                                      in_=zstage[:, zp, :]), 16)
                        tok_zst[zp] = tst
                    tok_tabI2b[par2] = tmm
                for rt in range(NRT):
                    gi = c * NRT + rt
                    tmn = opi("ve", cVE, lambda e, rt=rt: e.tensor_scalar(
                        col(34), sv[:, 24 + rt:25 + rt], 1.0 / D, None, op0=ALU.mult))
                    t2n = opi("ve", cVE, lambda e, rt=rt: e.tensor_scalar(
                        col(35), sv[:, 28 + rt:29 + rt], 1.0 / D, None, op0=ALU.mult))
                    waitl("ve", [tmn, t2n])
                    tvv = opi("ve", cVE, lambda e: e.tensor_mul(col(36), col(34), col(34)))
                    waitl("ve", [tvv])
                    tvs = opi("ve", cVE, lambda e: e.tensor_sub(col(37), col(35), col(36)))
                    waitl("ve", [tvs])
                    tva = opi("ve", cVE, lambda e: e.tensor_add(col(37), col(37), col(1)))
                    waitl("sc", [tva])
                    tsd = opi("sc", cSC, lambda e: e.activation(col(36), col(37), AF.Sqrt))
                    waitl("ve", [tsd])
                    trs = opi("ve", cVE, lambda e, gi=gi: e.reciprocal(
                        stats[:, gi, 0:1], col(36)))
                    waitl("ve", [trs])
                    tmr = opi("ve", cVE, lambda e, gi=gi: e.tensor_mul(
                        col(36), col(34), stats[:, gi, 0:1]))
                    waitl("ve", [tmr])
                    opi("ve", cVE, lambda e, gi=gi: e.tensor_scalar(
                        stats[:, gi, 1:2], col(36), -1.0, None, op0=ALU.mult))

            ckpt('M')
            tok_stats = (cVE, cVE.n)
            tok_z0 = (cZst[0], cZst[0].n)
            tok_z1 = (cZst[1], cZst[1].n)

            barrier()
            t_lr = opi("sy", cLnw, lambda e: e.dma_start(out=lnw_row[:], in_=lnw_in), 16)
            tok_bbank = [None, None]
            for hcx in range(NHC):
                bb = hcx % 2
                waitl("te", [t_or, t_lr, tok_stats, tok_bbank[bb]])
                tmm = opi("te", cTE, lambda e, hcx=hcx, bb=bb: e.matmul(
                    pall[:, bb, :], ones_row[:, :],
                    lnw_row[:, hcx * TC:(hcx + 1) * TC], start=True, stop=True))
                waitl("ve", [tmm])
                tok_bbank[bb] = opi("ve", cVE, lambda e, hcx=hcx, bb=bb: e.tensor_copy(
                    lnwb[:, hcx * TC:(hcx + 1) * TC], pall[:, bb, :]))
            tok_lnwb = (cVE, cVE.n)
            ckpt('Nb')

            tok_x1 = None
            tok_nxs = None
            tok_tb01 = [tok_lnwb, None]
            for gi in range(NTILE):
                waitl("sy", [tok_x1, tok_z0, tok_z1])
                tld = opi("sy", cX1ld, lambda e, gi=gi: e.dma_start(
                    out=x1stage[:], in_=x1scr[gi * 128:(gi + 1) * 128, :]), 16)
                waitl("ve", [tld, tok_stats, tok_nxs, tok_lnwb])
                tx1 = opi("ve", cVE, lambda e, gi=gi: e.tensor_scalar(
                    nxstage[:], x1stage[:], stats[:, gi, 0:1], stats[:, gi, 1:2],
                    op0=ALU.mult, op1=ALU.add))
                tok_x1 = tx1
                waitl("sc", [tx1])
                tsq = opi("sc", cSC, lambda e: e.activation(nsq16[:], nxstage[:], AF.Square))
                waitl("ve", [tsq])
                tr2 = opi("ve", cVE, lambda e: e.reduce_sum(col(23), nsq16[:], axis=AX.X))
                waitl("ve", [tr2])
                tms = opi("ve", cVE, lambda e: e.tensor_scalar(
                    col(27), col(23), 1.0 / D, col(1), op0=ALU.mult, op1=ALU.add))
                waitl("sc", [tms])
                tsd = opi("sc", cSC, lambda e: e.activation(col(26), col(27), AF.Sqrt))
                waitl("ve", [tsd])
                trs = opi("ve", cVE, lambda e: e.reciprocal(col(25), col(26)))
                waitl("ve", [trs])
                opi("ve", cVE, lambda e: e.tensor_scalar(
                    nxstage[:], nxstage[:], col(25), None, op0=ALU.mult))
                tnl = opi("ve", cVE, lambda e: e.tensor_mul(
                    nxstage[:], nxstage[:], lnwb[:]))
                for lt in range(NL):
                    bk = lt % 2
                    waitl("te", [tnl, tok_tb01[bk]])
                    ttp = opi("te", cTE, lambda e, lt=lt, bk=bk: e.transpose(
                        pall[:, bk, 0:128], nxstage[:, lt * 128:(lt + 1) * 128], ident[:]))
                    waitl("ve", [ttp])
                    tev = opi("ve", cVE, lambda e, lt=lt, gi=gi, bk=bk: e.tensor_copy(
                        nx16[:, lt, gi * 128:(gi + 1) * 128], pall[:, bk, 0:128]))
                    tok_tb01[bk] = tev
                tok_nxs = ttp
                ckpt('N.%d' % gi)
            tok_nx = (cVE, cVE.n)
            ckpt('N')

            barrier()
            tok_gb = [None, None]
            tok_ub = [None, None]
            tok_dsin = [None, None]
            tok_dsout = [None, None]
            tok_hst = [None, None]
            tok_gu_banks = [None] * 4
            tok_sig = [None, None]
            tok_h1 = [None, None]
            for ic in range(NI):
                par = ic % 2
                waitl("gp", [tok_gb[par]])
                tgl = opi("gp", cGb[par], lambda e, ic=ic, par=par: e.dma_start(
                    out=gbuf[:, par], in_=gateT[ic]), 16)
                waitl("gp", [tok_ub[par]])
                tul = opi("gp", cUb[par], lambda e, ic=ic, par=par: e.dma_start(
                    out=ubuf[:, par], in_=upT[ic]), 16)
                waitl("gp", [tok_dsout[par]])
                tdl = opi("gp", cDstg[par], lambda e, ic=ic, par=par: e.dma_start(
                    out=dstage[:, par, :], in_=downT[ic * 128:(ic + 1) * 128, :]), 16)
                waitl("sc", [tdl])
                tds = opi("sc", cDscr[par], lambda e, ic=ic, par=par: e.dma_start(
                    out=dscr[ic], in_=dstage[:, par, :]), 16)
                tok_dsout[par] = tds

                last_h = None
                for rc in range(NRC):
                    bkg = (rc % 2) * 2
                    bku = bkg + 1
                    waitl("te", [tgl, tul, tok_nx, tok_gu_banks[bkg], tok_gu_banks[bku]])
                    for lt in range(NL):
                        op("te", lambda e, lt=lt, par=par, rc=rc, bkg=bkg: e.matmul(
                            pall[:, bkg, 0:ATC], gbuf[:, par, lt, :],
                            nx16[:, lt, rc * ATC:(rc + 1) * ATC],
                            start=(lt == 0), stop=(lt == NL - 1)))
                    for lt in range(NL):
                        if lt < NL - 1:
                            op("te", lambda e, lt=lt, par=par, rc=rc, bku=bku: e.matmul(
                                pall[:, bku, 0:ATC], ubuf[:, par, lt, :],
                                nx16[:, lt, rc * ATC:(rc + 1) * ATC],
                                start=(lt == 0), stop=False))
                        else:
                            tmm = opi("te", cTE,
                                      lambda e, lt=lt, par=par, rc=rc, bku=bku:
                                      e.matmul(pall[:, bku, 0:ATC], ubuf[:, par, lt, :],
                                               nx16[:, lt, rc * ATC:(rc + 1) * ATC],
                                               start=False, stop=True))
                    if rc == NRC - 1:
                        tok_gb[par] = tmm
                        tok_ub[par] = tmm
                    sp = rc % 2
                    waitl("sc", [tmm, tok_sig[sp]])
                    tsg = opi("sc", cSC, lambda e, bkg=bkg, sp=sp: e.activation(
                        asig[:, sp, 0:ATC], pall[:, bkg, 0:ATC], AF.Sigmoid))
                    waitl("ve", [tsg, tok_h1[sp], tok_hst[par] if rc == 0 else None])
                    th1 = opi("ve", cVE, lambda e, bkg=bkg, sp=sp: e.tensor_mul(
                        ah1[:, sp, 0:ATC], asig[:, sp, 0:ATC], pall[:, bkg, 0:ATC]))
                    tok_sig[sp] = th1
                    waitl("ve", [th1])
                    th2 = opi("ve", cVE, lambda e, bku=bku, sp=sp, rc=rc, par=par:
                              e.tensor_mul(hstage[:, par, rc, :], ah1[:, sp, 0:ATC],
                                           pall[:, bku, 0:ATC]))
                    tok_h1[sp] = th2
                    tok_gu_banks[bkg] = th2
                    tok_gu_banks[bku] = th2
                    last_h = th2
                waitl("sy", [last_h])
                tho = opi("sy", cHsto[par], lambda e, ic=ic, par=par: e.dma_start(
                    out=hscr.rearrange("g p i r -> i p g r")[ic],
                    in_=hstage[:, par]), 16)
                tok_hst[par] = tho
            tok_A_h0 = (cHsto[0], cHsto[0].n)
            tok_A_h1 = (cHsto[1], cHsto[1].n)
            tok_A_te = (cTE, cTE.n)
            ckpt('A')

            barrier()
            HQ = max(1, D // 1024)
            HQW = D // HQ
            NH2 = HQW // TC
            tok_hslab = tok_A_te
            tok_dbuf = [None] * 6
            tok_bank8 = [None] * 8
            tok_x1q = [None] * 4
            tok_zq = [None, None]
            tok_sq16q = None
            tok_bz = None
            tok_outd = None
            for rg in range(NRG):
                waitl("sy", [tok_hslab, tok_A_h0, tok_A_h1])
                thl = opi("sy", cHsl, lambda e, rg=rg: e.dma_start(
                    out=hslab[:], in_=hscr[rg]), 16)
                tzro = None
                for q in range(8):
                    tzro = opi("ve", cVE, lambda e, q=q: e.memset(
                        sv[:, 24 + q:25 + q], 0.0))
                for hcq in range(HQ):
                    tok_txq = [None] * NGRT
                    for grt in range(NGRT):
                        gi = rg * NGRT + grt
                        xp = grt % 4
                        waitl("sy", [tok_x1q[xp], tok_z0, tok_z1])
                        tok_txq[grt] = opi("sy", cBx1[xp],
                                           lambda e, gi=gi, hcq=hcq, xp=xp: e.dma_start(
                                               out=x1q[:, xp, :],
                                               in_=x1scr[gi * 128:(gi + 1) * 128,
                                                         hcq * HQW:(hcq + 1) * HQW]), 16)
                    for ic in range(NI):
                        dp = ic % 6
                        qe = ("sy", "sc", "gp")[ic % 3]
                        waitl(qe, [tok_dbuf[dp], tok_dsout[0], tok_dsout[1]])
                        tdl = opi(qe, cDbuf[dp], lambda e, ic=ic, dp=dp, hcq=hcq:
                                  e.dma_start(out=dbufq[:, dp, :],
                                              in_=dscr[ic][:, hcq * HQW:(hcq + 1) * HQW]),
                                  16)
                        waitl("te", [tdl, thl])
                        for grt in range(NGRT):
                            for h2 in range(NH2):
                                bk = grt * NH2 + h2
                                if ic == 0:
                                    waitl("te", [tok_bank8[bk]])
                                if grt == NGRT - 1 and h2 == NH2 - 1:
                                    tmm = opi("te", cTE,
                                              lambda e, ic=ic, grt=grt, h2=h2, bk=bk, dp=dp:
                                              e.matmul(pall[:, bk, :],
                                                       hslab[:, ic, grt * 128:(grt + 1) * 128],
                                                       dbufq[:, dp, h2 * TC:(h2 + 1) * TC],
                                                       start=(ic == 0), stop=(ic == NI - 1)))
                                else:
                                    op("te", lambda e, ic=ic, grt=grt, h2=h2, bk=bk, dp=dp:
                                       e.matmul(pall[:, bk, :],
                                                hslab[:, ic, grt * 128:(grt + 1) * 128],
                                                dbufq[:, dp, h2 * TC:(h2 + 1) * TC],
                                                start=(ic == 0), stop=(ic == NI - 1)))
                        tok_dbuf[dp] = tmm
                    for grt in range(NGRT):
                        gi = rg * NGRT + grt
                        xp = grt % 4
                        zp = grt % 2
                        waitl("ve", [tok_txq[grt], tok_stats, tok_zq[zp], tzro])
                        tx1 = opi("ve", cVE, lambda e, gi=gi, xp=xp: e.tensor_scalar(
                            x1q[:, xp, :], x1q[:, xp, :], stats[:, gi, 0:1],
                            stats[:, gi, 1:2], op0=ALU.mult, op1=ALU.add))
                        for h2 in range(NH2):
                            bk = grt * NH2 + h2
                            waitl("ve", [tmm])
                            tz = opi("ve", cVE, lambda e, bk=bk, h2=h2, xp=xp, zp=zp:
                                     e.tensor_add(zq[:, zp, h2 * TC:(h2 + 1) * TC],
                                                  pall[:, bk, :],
                                                  x1q[:, xp, h2 * TC:(h2 + 1) * TC]))
                            tok_bank8[bk] = tz
                        tok_x1q[xp] = tz
                        tr1 = opi("ve", cVE, lambda e, zp=zp: e.reduce_sum(
                            col(22), zq[:, zp, :], axis=AX.X))
                        ta1 = opi("ve", cVE, lambda e, grt=grt: e.tensor_add(
                            sv[:, 24 + grt:25 + grt], sv[:, 24 + grt:25 + grt], col(22)))
                        waitl("sc", [tz, tok_sq16q])
                        tsq = opi("sc", cSC, lambda e, zp=zp: e.activation(
                            sq16q[:], zq[:, zp, :], AF.Square))
                        waitl("ve", [tsq])
                        tr2 = opi("ve", cVE, lambda e: e.reduce_sum(
                            col(23), sq16q[:], axis=AX.X))
                        tok_sq16q = tr2
                        ta2 = opi("ve", cVE, lambda e, grt=grt: e.tensor_add(
                            sv[:, 28 + grt:29 + grt], sv[:, 28 + grt:29 + grt], col(23)))
                        waitl("sy", [ta1, tr2])
                        tzs = opi("sy", cZq[zp], lambda e, gi=gi, hcq=hcq, zp=zp:
                                  e.dma_start(out=x1scr[gi * 128:(gi + 1) * 128,
                                                        hcq * HQW:(hcq + 1) * HQW],
                                              in_=zq[:, zp, :]), 16)
                        tok_zq[zp] = tzs
                tok_z0 = (cZq[0], cZq[0].n)
                tok_z1 = (cZq[1], cZq[1].n)
                for grt in range(NGRT):
                    gi = rg * NGRT + grt
                    tmn = opi("ve", cVE, lambda e, grt=grt: e.tensor_scalar(
                        col(34), sv[:, 24 + grt:25 + grt], 1.0 / D, None, op0=ALU.mult))
                    t2n = opi("ve", cVE, lambda e, grt=grt: e.tensor_scalar(
                        col(35), sv[:, 28 + grt:29 + grt], 1.0 / D, None, op0=ALU.mult))
                    tvv = opi("ve", cVE, lambda e: e.tensor_mul(col(36), col(34), col(34)))
                    tvs = opi("ve", cVE, lambda e: e.tensor_sub(col(37), col(35), col(36)))
                    tva = opi("ve", cVE, lambda e: e.tensor_add(col(37), col(37), col(1)))
                    waitl("sc", [tva])
                    tsd = opi("sc", cSC, lambda e: e.activation(col(36), col(37), AF.Sqrt))
                    waitl("ve", [tsd])
                    trs = opi("ve", cVE, lambda e, gi=gi: e.reciprocal(
                        statsB[:, gi, 0:1], col(36)))
                    tmr = opi("ve", cVE, lambda e, gi=gi: e.tensor_mul(
                        col(36), col(34), statsB[:, gi, 0:1]))
                    opi("ve", cVE, lambda e, gi=gi: e.tensor_scalar(
                        statsB[:, gi, 1:2], col(36), -1.0, None, op0=ALU.mult))
                tok_statsB = (cVE, cVE.n)
                for grt in range(NGRT):
                    gi = rg * NGRT + grt
                    waitl("sy", [tok_bz, tok_z0, tok_z1])
                    tbl = opi("sy", cX1ld, lambda e, gi=gi: e.dma_start(
                        out=bz[:], in_=x1scr[gi * 128:(gi + 1) * 128, :]), 16)
                    waitl("ve", [tbl, tok_statsB, tok_outd])
                    tfin = opi("ve", cVE, lambda e, gi=gi: e.tensor_scalar(
                        bz[:], bz[:], statsB[:, gi, 0:1], statsB[:, gi, 1:2],
                        op0=ALU.mult, op1=ALU.add))
                    waitl("sy", [tfin])
                    tou = opi("sy", cOut, lambda e, gi=gi: e.dma_start(
                        out=out_d[gi * 128:(gi + 1) * 128, :], in_=bz[:]), 16)
                    tok_outd = tou
                    tok_bz = tou
                tok_hslab = (cTE, cTE.n)

            waitl("sy", [(cOut, cOut.n)])
            op("sy", lambda e: e.nop())

        try:
            _emit_all()
        except _Stop:
            for ek in Emit.ENGINES:
                for ct in ALLCT:
                    em.wait(ek, (ct, ct.n))
            tdmy = opi("sy", cOut, lambda e: e.dma_start(
                out=out_d[0:128, 0:128], in_=ident[:]), 16)
            em.wait("sy", tdmy)


        block = E(nc.Block())

        @block.sync
        def _(eng):
            em.replay("sy", eng)

        @block.scalar
        def _(eng):
            em.replay("sc", eng)

        @block.vector
        def _(eng):
            em.replay("ve", eng)

        @block.tensor
        def _(eng):
            em.replay("te", eng)

        @block.gpsimd
        def _(eng):
            em.replay("gp", eng)

    return nc


def make_tables_host(D, KP):
    li = np.arange(D, dtype=np.int64)
    ki = np.arange(KP, dtype=np.int64)
    mf = np.mod(np.outer(li, ki), D)
    mi = np.mod(np.outer(ki, li), D)
    midx_f_flat = np.stack([(mf + 3 * D // 4) % D, (mf + D // 2) % D]).astype(np.uint16)
    NL = D // 128
    NKl = KP // 128
    midx_f = np.ascontiguousarray(
        midx_f_flat.reshape(2, NL, 128, NKl, 128).transpose(0, 3, 2, 1, 4))
    midx_i = np.stack([(mi + 3 * D // 4) % D, (mi + D // 2) % D]).astype(np.uint16)
    alpha = np.where((ki == 0) | (ki == D // 2), 1.0, 2.0)
    alpha[ki > D // 2] = 0.0
    alpha = (alpha / D).astype(np.float64)
    NK = KP // 128
    a_l = alpha.reshape(NK, 128).T
    alpha_in = np.stack([a_l, -a_l]).astype(np.float32)
    return midx_f, midx_i, alpha_in


def kernel(L0_attn_x, L0_ffn_ln_w, L0_ffn_gate_w, L0_ffn_up_w, L0_ffn_down_w):
    x = np.ascontiguousarray(np.asarray(L0_attn_x, dtype=np.float32))
    B, S, D = x.shape
    I = L0_ffn_gate_w.shape[0]
    RC = (B * S) // N_CORES
    SL = S // N_CORES

    NL_, NI_ = D // 128, I // 128
    gateT = np.ascontiguousarray(np.asarray(L0_ffn_gate_w, np.float32).T.reshape(
        NL_, 128, NI_, 128).transpose(2, 1, 0, 3))
    upT = np.ascontiguousarray(np.asarray(L0_ffn_up_w, np.float32).T.reshape(
        NL_, 128, NI_, 128).transpose(2, 1, 0, 3))
    downT = np.ascontiguousarray(np.asarray(L0_ffn_down_w, np.float32).T)
    lnw = np.ascontiguousarray(np.asarray(L0_ffn_ln_w, np.float32).reshape(1, D))
    KP = ((D // 2 + 1 + 127) // 128) * 128
    midx_f, midx_i, alpha_in = make_tables_host(D, KP)
    ident = np.eye(128, dtype=np.float32)

    in_maps = []
    for c in range(N_CORES):
        xs = np.ascontiguousarray(x[:, c * SL:(c + 1) * SL, :].reshape(RC, D))
        in_maps.append(dict(
            x_in=xs, gateT=gateT, upT=upT, downT=downT, lnw_in=lnw,
            midx_f=midx_f, midx_i=midx_i, alpha_in=alpha_in, ident_in=ident))

    nc = build_program(D=D, I=I, RC=RC, CH=512)
    res = run_bass_kernel_spmd(nc, in_maps, core_ids=list(range(N_CORES)))

    out = np.empty((B, S, D), dtype=np.float32)
    for c in range(N_CORES):
        out[:, c * SL:(c + 1) * SL, :] = res.results[c]["out_d"].reshape(B, SL, D)
    return out


# version 14
# speedup vs baseline: 1.1051x; 1.0713x over previous
import math
import numpy as np
from contextlib import ExitStack

import concourse.bass as bass
import concourse.mybir as mybir
from concourse.bass_utils import run_bass_kernel_spmd

F32 = mybir.dt.float32
BF16 = mybir.dt.bfloat16
U16 = mybir.dt.uint16
AF = mybir.ActivationFunctionType
ALU = mybir.AluOpType
AX = mybir.AxisListType

N_CORES = 8


class Ct:
    def __init__(self, sem):
        self.sem = sem
        self.n = 0


class Emit:
    ENGINES = ("sy", "sc", "ve", "te", "gp")

    def __init__(self):
        self.prog = {k: [] for k in self.ENGINES}
        self._lastwait = {}
        self.selfct = {}

    def _auto(self, ek):
        ct = self.selfct.get(ek)
        if ct is not None and ct.n > 0:
            self.wait(ek, (ct, ct.n))

    def op(self, ek, f):
        self._auto(ek)
        self.prog[ek].append(f)

    def opi(self, ek, ct, f, k=1):
        self._auto(ek)
        ct.n += k
        sem = ct.sem
        def g(e, f=f, sem=sem, k=k):
            f(e).then_inc(sem, k)
        self.prog[ek].append(g)
        return (ct, ct.n)

    def wait(self, ek, tok):
        if tok is None:
            return
        ct, val = tok
        if val <= 0:
            return
        key = (ek, id(ct))
        if self._lastwait.get(key, 0) >= val:
            return
        self._lastwait[key] = val
        sem = ct.sem
        self.prog[ek].append(lambda e, sem=sem, val=val: e.wait_ge(sem, val))

    def replay(self, ek, eng):
        for f in self.prog[ek]:
            f(eng)


def build_program(D=4096, I=14336, RC=2048, CH=512, stop_after=None):
    KF = D // 2 + 1
    NK = (KF + 127) // 128
    KP = NK * 128
    NL = D // 128
    NCH = RC // CH
    NRT = CH // 128
    NI = I // 128
    TC = 512
    NHC = D // TC
    NTC = (KF + TC - 1) // TC
    NTT = NK
    CW = NTC * TC
    RG = min(512, RC)
    NRG = RC // RG
    NGRT = RG // 128
    NTILE = RC // 128
    ATC = min(TC, RC)
    NRC = RC // ATC
    SC_INV = 2.0 * math.pi / D

    nc = bass.Bass("TRN2", target_bir_lowering=False)

    x_in = nc.dram_tensor("x_in", [RC, D], F32, kind="ExternalInput").ap()
    gateT = nc.dram_tensor("gateT", [I // 128, 128, D // 128, 128], F32,
                           kind="ExternalInput").ap()
    upT = nc.dram_tensor("upT", [I // 128, 128, D // 128, 128], F32,
                         kind="ExternalInput").ap()
    downT = nc.dram_tensor("downT", [I, D], F32, kind="ExternalInput").ap()
    lnw_in = nc.dram_tensor("lnw_in", [1, D], F32, kind="ExternalInput").ap()
    midx_f = nc.dram_tensor("midx_f", [2, KP // 128, 128, D // 128, 128], U16, kind="ExternalInput").ap()
    midx_i = nc.dram_tensor("midx_i", [2, KP, D], U16, kind="ExternalInput").ap()
    alpha_in = nc.dram_tensor("alpha_in", [2, 128, NK], F32, kind="ExternalInput").ap()
    ident_in = nc.dram_tensor("ident_in", [128, 128], F32, kind="ExternalInput").ap()
    out_d = nc.dram_tensor("out_d", [RC, D], F32, kind="ExternalOutput").ap()

    tabF = nc.dram_tensor("tabF", [2, NK, 128, NL, 128], BF16, kind="Internal").ap()
    tabI6 = nc.dram_tensor("tabI6", [NTC, 128, NK, TC], BF16, kind="Internal").ap()
    tabFS = nc.dram_tensor("tabFS", [NK, 128, NTT, 128], BF16, kind="Internal").ap()
    tabI2 = nc.dram_tensor("tabI2", [NHC, 128, NK, 2, TC], BF16, kind="Internal").ap()
    fscr = nc.dram_tensor("fscr", [2, 128, NK, CH], BF16, kind="Internal").ap()
    x1scr = nc.dram_tensor("x1scr", [RC, D], F32, kind="Internal").ap()
    hscr = nc.dram_tensor("hscr", [NRG, 128, NI, RG], BF16, kind="Internal").ap()
    dscr = nc.dram_tensor("dscr", [NI, 128, D], BF16, kind="Internal").ap()

    with ExitStack() as ctx:
        E = ctx.enter_context

        SB_BASE = 16512
        SB_TOP = nc.SBUF_PARTITION_SIZE_BYTES - 32

        def at(name, shape, dtype, off):
            nbytes = int(np.prod(shape[1:])) * mybir.dt.size(dtype)
            assert SB_BASE + off + nbytes <= SB_TOP, (name, off, nbytes)
            return nc.alloc_sbuf_tensor_at(name, shape, dtype, offset=SB_BASE + off)

        K = 1024
        ident = at("ident", [128, 128], F32, 0)
        alpha = at("alpha", [128, 2, NK], F32, 512)
        sv = at("sv", [128, 64], F32, 672)
        stats = at("stats", [128, NTILE, 2], F32, 928)
        statsB = at("statsB", [128, NTILE, 2], F32, 1056)
        ones_row = at("ones_row", [1, 128], F32, 1184)
        PH = 1728
        BUDGET = SB_TOP - SB_BASE - PH

        ustage = at("ustage", [128, 2, max(KP, D)], U16, PH + 0)
        t32 = at("t32", [128, max(KP, D)], F32, PH + 16 * K)
        tf32 = at("tf32", [128, max(KP, D)], F32, PH + 32 * K)
        tb16 = at("tb16", [128, 2, max(KP, D)], BF16, PH + 48 * K)
        tbs16 = at("tbs16", [128, NTT * 128], BF16, PH + 64 * K)

        xstage = at("xstage", [128, D], F32, PH + 0)
        s32 = at("s32", [128, D], F32, PH + 0)
        slotAB = at("slotAB", [128, NL, CH], BF16, PH + 16 * K)
        fslot = at("fslot", [128, 2, NK, CH], BF16, PH + 48 * K)
        OFF_P = 48 * K + 2 * NK * CH * 2
        Pslot = at("Pslot", [128, NK, CH], BF16, PH + OFF_P)
        OFF_CORR = OFF_P + NK * CH * 2
        corrslot = at("corrslot", [128, NRT, CW], BF16, PH + OFF_CORR)
        OFF_TMP = OFF_CORR + NRT * D * 2
        e16 = at("e16", [128, D], BF16, PH + OFF_TMP)
        mask16 = at("mask16", [128, D], BF16, PH + OFF_TMP + D * 2)
        ystage = at("ystage", [128, 4, TC], F32, PH + OFF_TMP)
        xlc = at("xlc", [128, NRT, TC], F32, PH + OFF_TMP + 8 * K)
        OFF_TB = OFF_TMP + 16 * K
        zstage = at("zstage", [128, 2, TC], F32, PH + OFF_TB)
        sq16 = at("sq16", [128, TC], BF16, PH + OFF_TB + 4 * K)
        OFF_TABF = OFF_TB + 5 * K + 256
        tabFbuf = at("tabFbuf", [128, 2, NL, 128], BF16, PH + OFF_TABF)
        OFF_TABI = OFF_TABF + 2 * NL * 128 * 2
        tabIbuf = at("tabIbuf", [128, 2, NK, TC], BF16, PH + OFF_TABI)
        tabI2buf = at("tabI2buf", [128, NK, 2, TC], BF16, PH + OFF_TABI)
        NKA = min(NK, (NL * CH * 2) // (2 * TC * 2))
        NKB = NK - NKA
        tabI2bA = at("tabI2bA", [128, NKA, 2, TC], BF16, PH + 16 * K)
        tabI2bB = (at("tabI2bB", [128, NKB, 2, TC], BF16, PH + OFF_P)
                   if NKB > 0 else None)
        OFF_FRL = OFF_TABI + 2 * NK * TC * 2
        freload = at("freload", [128, 2, CH], BF16, PH + OFF_FRL)
        M_END = OFF_FRL + 2 * CH * 2
        assert M_END <= BUDGET, ("M overlay", M_END, BUDGET)

        nx16 = at("nx16", [128, NL, RC], BF16, PH + 0)
        OFF_N = NL * RC * 2
        x1stage = at("x1stage", [128, D], F32, PH + OFF_N)
        nxstage = at("nxstage", [128, D], F32, PH + OFF_N + 16 * K)
        lnwb = at("lnwb", [128, D], F32, PH + OFF_N + 32 * K)
        lnw_row = at("lnw_row", [1, D], F32, PH + OFF_N + 48 * K)
        nsq16 = at("nsq16", [128, D], BF16, PH + OFF_N + 64 * K)
        assert OFF_N + 72 * K <= BUDGET

        gbuf = at("gbuf", [128, 2, NL, 128], BF16, PH + OFF_N)
        ubuf = at("ubuf", [128, 2, NL, 128], BF16, PH + OFF_N + 16 * K)
        hstage = at("hstage", [128, 2, NRC, ATC], BF16, PH + OFF_N + 32 * K)
        dstage = at("dstage", [128, 2, D], BF16, PH + OFF_N + 40 * K)
        asig = at("asig", [128, 2, TC], F32, PH + OFF_N + 56 * K)
        ah1 = at("ah1", [128, 2, TC], F32, PH + OFF_N + 60 * K)
        assert OFF_N + 64 * K <= BUDGET

        HQW_ = D // max(1, D // 1024)
        hslab = at("hslab", [128, NI, RG], BF16, PH + 0)
        OFF_B = NI * RG * 2
        dbufq = at("dbufq", [128, 6, HQW_], BF16, PH + OFF_B)
        x1q = at("x1q", [128, 4, HQW_], F32, PH + OFF_B + 12 * K)
        zq = at("zq", [128, 2, HQW_], F32, PH + OFF_B + 28 * K)
        sq16q = at("sq16q", [128, HQW_], BF16, PH + OFF_B + 36 * K)
        bz = at("bz", [128, D], F32, PH + OFF_B + 38 * K + 256)
        assert OFF_B + 55 * K <= BUDGET

        pall = E(nc.psum_tensor("pall", [128, 8, TC], F32))

        def mkct(name):
            return Ct(E(nc.semaphore(name)))

        cInit = mkct("cInit")
        cLnw = mkct("cLnw")
        cUst = [mkct("cUst0"), mkct("cUst1")]
        cTbst = [mkct("cTbst0"), mkct("cTbst1")]
        cTbsS = mkct("cTbsS")
        cX = mkct("cX")
        cTabF = [mkct("cTabF0"), mkct("cTabF1")]
        cFsp = [mkct("cFsp0"), mkct("cFsp1")]
        cTabI = [mkct("cTabI0"), mkct("cTabI1")]
        cTabI2 = [mkct("cTabI2a"), mkct("cTabI2b")]
        cXlc = [mkct("cXlc%d" % i) for i in range(NRT)]
        cZst = [mkct("cZst0"), mkct("cZst1")]
        cZq = [mkct("cZq0"), mkct("cZq1")]
        cFrl = mkct("cFrl")
        cX1ld = mkct("cX1ld")
        cHsto = [mkct("cHsto0"), mkct("cHsto1")]
        cDscr = [mkct("cDscr0"), mkct("cDscr1")]
        cHsl = mkct("cHsl")
        cDbuf = [mkct("cDbuf%d" % i) for i in range(6)]
        cBx1 = [mkct("cBx1%d" % i) for i in range(4)]
        cOut = mkct("cOut")
        cGb = [mkct("cGb0"), mkct("cGb1")]
        cUb = [mkct("cUb0"), mkct("cUb1")]
        cDstg = [mkct("cDstg0"), mkct("cDstg1")]
        cTE = mkct("cTE")
        cVE = mkct("cVE")
        cSC = mkct("cSC")
        cGP = mkct("cGP")

        ALLCT = ([cInit, cLnw, cX, cFrl, cX1ld, cHsl, cOut,
                  cTE, cVE, cSC, cGP] + cUst + cTbst + [cTbsS] + cTabF + cFsp + cTabI + cTabI2 +
                 cXlc + cZst + cZq + cHsto + cDbuf + cBx1 + cGb + cUb + cDstg + cDscr)

        em = Emit()
        em.selfct = {"ve": cVE, "sc": cSC, "gp": cGP}
        op, opi = em.op, em.opi

        def barrier():
            for ek in Emit.ENGINES:
                for ct in ALLCT:
                    em.wait(ek, (ct, ct.n))

        class _Stop(Exception):
            pass

        def ckpt(name):
            if stop_after == name:
                raise _Stop()

        def waitl(ek, toks):
            for t in toks:
                em.wait(ek, t)

        def col(i):
            return sv[:, i:i + 1]

        def _emit_all():
            t_id = opi("sy", cInit, lambda e: e.dma_start(out=ident[:], in_=ident_in), 16)
            t_al = opi("sy", cInit, lambda e: e.dma_start(
                out=alpha[:], in_=alpha_in.rearrange("a p k -> p a k")), 16)
            t_init = (cInit, cInit.n)
            t_cpi = opi("ve", cVE, lambda e: e.memset(col(0), -math.pi))
            t_eps = opi("ve", cVE, lambda e: e.memset(col(1), 1e-5))
            t_or = opi("ve", cVE, lambda e: e.memset(ones_row[:], 1.0))
            for _i, _w in enumerate([1.0, 2.0, 2.0, 2.0, 1.0, 0.0, 0.0, 0.0]):
                opi("ve", cVE, lambda e, _i=_i, _w=_w: e.memset(
                    sv[:, 48 + _i:49 + _i], _w))

            tok_ust = [None, None]
            tok_tb = [None, None]
            tok_tbs = None
            tok_t32 = None
            tok_tf32 = None
            for tab in range(2):
                for kt in range(NK):
                    par = kt % 2
                    waitl("sy", [tok_ust[par]])
                    tld = opi("sy", cUst[par], lambda e, tab=tab, kt=kt, par=par:
                              e.dma_start(
                        out=ustage[:, par, 0:NL * 128].rearrange(
                            "p (l j) -> p l j", j=128),
                        in_=midx_f[tab, kt]), 16)
                    waitl("ve", [tld, tok_t32])
                    tcv = opi("ve", cVE, lambda e, par=par: e.tensor_copy(
                        t32[:, 0:NL * 128], ustage[:, par, 0:NL * 128]))
                    tok_ust[par] = tcv
                    waitl("sc", [tcv, t_cpi, tok_tb[par]])
                    tsn = opi("sc", cSC, lambda e, par=par: e.activation(
                        tb16[:, par, 0:NL * 128], t32[:, 0:NL * 128], AF.Sin,
                        bias=col(0), scale=SC_INV))
                    tok_t32 = tsn
                    waitl("sy", [tsn])
                    tst = opi("sy", cTbst[par], lambda e, tab=tab, kt=kt, par=par:
                              e.dma_start(
                        out=tabF[tab, kt].rearrange("p l j -> p l j"),
                        in_=tb16[:, par, 0:NL * 128].rearrange(
                            "p (l j) -> p l j", j=128)), 16)
                    tok_tb[par] = tst
                    if tab == 0:
                        waitl("ve", [tsn, tok_tbs])
                        tts = opi("ve", cVE, lambda e, par=par: e.tensor_scalar(
                            tbs16[:], tb16[:, par, 0:NTT * 128], 2.0, None,
                            op0=ALU.mult))
                        opi("ve", cVE, lambda e: e.tensor_scalar(
                            tbs16[0:1, 0:128], tbs16[0:1, 0:128], 0.5, None,
                            op0=ALU.mult))
                        tp2 = opi("ve", cVE, lambda e: e.tensor_scalar(
                            tbs16[0:1, (NTT - 1) * 128:(NTT - 1) * 128 + 128],
                            tbs16[0:1, (NTT - 1) * 128:(NTT - 1) * 128 + 128],
                            0.5, None, op0=ALU.mult))
                        waitl("sy", [tp2])
                        tok_tbs = opi("sy", cTbsS, lambda e, kt=kt: e.dma_start(
                            out=tabFS[kt],
                            in_=tbs16.rearrange("p (t j) -> p t j", j=128)), 16)
            for tab in range(2):
                for kt in range(NK):
                    par = kt % 2
                    waitl("sy", [tok_ust[par]])
                    tld = opi("sy", cUst[par], lambda e, tab=tab, kt=kt, par=par: e.dma_start(
                        out=ustage[:, par, 0:D],
                        in_=midx_i[tab, kt * 128:(kt + 1) * 128, :]), 16)
                    waitl("ve", [tld, tok_t32])
                    tcv = opi("ve", cVE, lambda e, par=par: e.tensor_copy(
                        t32[:, 0:D], ustage[:, par, 0:D]))
                    tok_ust[par] = tcv
                    waitl("sc", [tcv, tok_tf32])
                    tsn = opi("sc", cSC, lambda e: e.activation(
                        tf32[:, 0:D], t32[:, 0:D], AF.Sin, bias=col(0), scale=SC_INV))
                    tok_t32 = tsn
                    waitl("ve", [tsn, t_al, tok_tb[par]])
                    tmu = opi("ve", cVE, lambda e, tab=tab, kt=kt, par=par: e.tensor_scalar(
                        tb16[:, par, 0:D], tf32[:, 0:D], alpha[:, tab, kt:kt + 1], None,
                        op0=ALU.mult))
                    tok_tf32 = tmu
                    waitl("sy", [tmu])
                    if tab == 0:
                        opi("sy", cTbst[par], lambda e, kt=kt, par=par: e.dma_start(
                            out=tabI6[:, :, kt].rearrange("c p j -> p c j"),
                            in_=tb16[:, par, 0:min(CW, D)].rearrange(
                                "p (c j) -> p c j", j=TC)), 16)
                    t2 = opi("sy", cTbst[par], lambda e, tab=tab, kt=kt, par=par: e.dma_start(
                        out=tabI2[:, :, kt, tab].rearrange("c p j -> p c j"),
                        in_=tb16[:, par, 0:D].rearrange("p (c j) -> p c j", j=TC)), 16)
                    tok_tb[par] = t2
            tok_T0 = (cTbst[0], cTbst[0].n)
            tok_T1 = (cTbst[1], cTbst[1].n)
            ckpt("T")


            barrier()
            tok_xstage = None
            tok_tabFb = [tok_T0, tok_T1]
            tok_tabIb = [tok_T0, tok_T1]
            tok_tabI2b = [tok_T1, None]
            tok_fsp = [None, None]
            tok_zst = [None, None]
            tok_s32 = None
            tok_slotAB_w = None
            tok_sT_last = None
            tok_sq16 = None
            tok_xlc = [None] * NRT
            tok_frl = None
            tok_cbank = [None] * NRT

            for c in range(NCH):
                rb = c * CH
                tok_chunk_sc = (cSC, cSC.n)
                tok_chunk_gp = (cGP, cGP.n)
                tok_slot_readers = (cTE, cTE.n)

                tok_tb01 = [None, None]
                for rt in range(NRT):
                    waitl("gp", [tok_xstage])
                    txd = opi("gp", cX, lambda e, rb=rb, rt=rt: e.dma_start(
                        out=xstage[:], in_=x_in[rb + rt * 128: rb + (rt + 1) * 128, :]), 16)
                    for lt in range(NL):
                        bk = lt % 2
                        waitl("te", [txd, tok_tb01[bk], tok_slot_readers, t_init])
                        ttp = opi("te", cTE, lambda e, lt=lt, bk=bk: e.transpose(
                            pall[:, bk, 0:128], xstage[:, lt * 128:(lt + 1) * 128],
                            ident[:]))
                        waitl("ve", [ttp, tok_slotAB_w])
                        tev = opi("ve", cVE, lambda e, lt=lt, rt=rt, bk=bk: e.tensor_copy(
                            slotAB[:, lt, rt * 128:(rt + 1) * 128], pall[:, bk, 0:128]))
                        tok_tb01[bk] = tev
                    tok_xstage = ttp
                tok_xT = (cVE, cVE.n)
                ckpt('M3.%d' % c)

                tok_fbank = [tok_tb01[0], tok_tb01[1], None, None]
                for half in range(2):
                    for kt in range(NK):
                        par = (half * NK + kt) % 2
                        waitl("sc", [tok_tabFb[par]])
                        tld = opi("sc", cTabF[par], lambda e, half=half, kt=kt, par=par:
                                  e.dma_start(out=tabFbuf[:, par],
                                              in_=tabF[half, kt]), 16)
                        gq = (half * NK + kt) % 4
                        bk = gq
                        waitl("te", [tld, tok_xT, tok_fbank[gq]])
                        for lt in range(NL):
                            if lt < NL - 1:
                                op("te", lambda e, par=par, lt=lt, bk=bk: e.matmul(
                                    pall[:, bk, 0:CH], tabFbuf[:, par, lt, :],
                                    slotAB[:, lt, :], start=(lt == 0), stop=False))
                            else:
                                tmm = opi("te", cTE, lambda e, par=par, lt=lt, bk=bk:
                                          e.matmul(pall[:, bk, 0:CH],
                                                   tabFbuf[:, par, lt, :],
                                                   slotAB[:, lt, :],
                                                   start=False, stop=True))
                        tok_tabFb[par] = tmm
                        waitl("ve", [tmm])
                        tev = opi("ve", cVE, lambda e, half=half, kt=kt, bk=bk:
                                  e.tensor_copy(fslot[:, half, kt, :], pall[:, bk, 0:CH]))
                        tok_fbank[gq] = tev
                tok_f = (cVE, cVE.n)
                ckpt('M4.%d' % c)

                waitl("sy", [tok_f, tok_frl])
                for half in range(2):
                    tok_fsp[half] = opi("sy", cFsp[half], lambda e, half=half: e.dma_start(
                        out=fscr[half], in_=fslot[:, half]), 16)

                tok_ptmp = [None] * 4
                for kt in range(NK):
                    sl0, sl1 = (kt % 2) * 2, (kt % 2) * 2 + 1
                    waitl("sc", [tok_f, tok_ptmp[sl0], tok_chunk_gp])
                    opi("sc", cSC, lambda e, kt=kt, sl0=sl0: e.activation(
                        ystage[:, sl0, 0:CH], fslot[:, 0, kt, :], AF.Square))
                    ts1 = opi("sc", cSC, lambda e, kt=kt, sl1=sl1: e.activation(
                        ystage[:, sl1, 0:CH], fslot[:, 1, kt, :], AF.Square))
                    waitl("ve", [ts1, tok_slot_readers])
                    tad = opi("ve", cVE, lambda e, kt=kt, sl0=sl0, sl1=sl1: e.tensor_add(
                        Pslot[:, kt, :], ystage[:, sl0, 0:CH], ystage[:, sl1, 0:CH]))
                    tok_ptmp[sl0] = tad
                    tok_ptmp[sl1] = tad
                tok_P = (cVE, cVE.n)
                ckpt('M5.%d' % c)

                for tc in range(NTC):
                    par = tc % 2
                    waitl("sy", [tok_tabIb[par], tok_tabI2b[0], tok_tabI2b[1],
                                 (cTabI2[0], cTabI2[0].n), (cTabI2[1], cTabI2[1].n)])
                    tld = opi("sy", cTabI[par], lambda e, tc=tc, par=par: e.dma_start(
                        out=tabIbuf[:, par], in_=tabI6[tc]), 16)
                    for rt in range(NRT):
                        bk = 4 + rt
                        waitl("te", [tld, tok_P, tok_cbank[rt]])
                        for kt in range(NK):
                            if kt < NK - 1:
                                op("te", lambda e, kt=kt, rt=rt, par=par, bk=bk: e.matmul(
                                    pall[:, bk, :], Pslot[:, kt, rt * 128:(rt + 1) * 128],
                                    tabIbuf[:, par, kt, :], start=(kt == 0), stop=False))
                            else:
                                tmm = opi("te", cTE,
                                          lambda e, kt=kt, rt=rt, par=par, bk=bk:
                                          e.matmul(pall[:, bk, :],
                                                   Pslot[:, kt, rt * 128:(rt + 1) * 128],
                                                   tabIbuf[:, par, kt, :],
                                                   start=False, stop=True))
                        waitl("ve", [tmm, tok_chunk_sc])
                        tev = opi("ve", cVE, lambda e, rt=rt, tc=tc, bk=bk: e.tensor_copy(
                            corrslot[:, rt, tc * TC:(tc + 1) * TC], pall[:, bk, :]))
                        tok_cbank[rt] = tev
                    tok_tabIb[par] = (cTE, cTE.n)
                if CW > KF:
                    for rt in range(NRT):
                        opi("ve", cVE, lambda e, rt=rt: e.memset(
                            corrslot[:, rt, KF:CW], -3.0e38))
                tok_corr = (cVE, cVE.n)
                ckpt('M6.%d' % c)

                for rt in range(NRT):
                    mx8 = sv[:, 6:14]
                    e8 = sv[:, 14:22]
                    waitl("ve", [tok_corr, tok_s32, (cSC, cSC.n)])
                    tm8 = opi("ve", cVE, lambda e, rt=rt, mx8=mx8: e.max(
                        out=mx8, in_=corrslot[:, rt, 0:CW]))
                    waitl("ve", [tm8])
                    tng = opi("ve", cVE, lambda e, mx8=mx8: e.tensor_scalar(
                        col(3), mx8[:, 0:1], -1.0, None, op0=ALU.mult))
                    waitl("sc", [tng])
                    tex = opi("sc", cSC, lambda e, mx8=mx8, e8=e8: e.activation(
                        e8, mx8, AF.Exp, bias=col(3)))
                    waitl("ve", [tex])
                    tzw = opi("ve", cVE, lambda e, e8=e8: e.tensor_mul(
                        sv[:, 40:48], e8, sv[:, 48:56]))
                    waitl("ve", [tzw])
                    tzs = opi("ve", cVE, lambda e: e.reduce_sum(
                        col(5), sv[:, 40:48], axis=AX.X))
                    waitl("ve", [tzs])
                    tiz = opi("ve", cVE, lambda e: e.reciprocal(col(4), col(5)))
                    te16 = opi("sc", cSC, lambda e, rt=rt: e.activation(
                        e16[:, 0:CW], corrslot[:, rt, 0:CW], AF.Exp, bias=col(3)))
                    tmk = opi("ve", cVE, lambda e, rt=rt, mx8=mx8: e.tensor_scalar(
                        mask16[:, 0:CW], corrslot[:, rt, 0:CW], mx8[:, 4:5], None,
                        op0=ALU.is_ge))
                    waitl("ve", [te16, tiz, tmk])
                    ts32 = opi("ve", cVE, lambda e: e.scalar_tensor_tensor(
                        s32[:, 0:CW], e16[:, 0:CW], col(4), mask16[:, 0:CW],
                        op0=ALU.mult, op1=ALU.mult))
                    for lt in range(NTT):
                        bk = lt % 2
                        waitl("te", [ts32, tok_tb01[bk]])
                        ttp = opi("te", cTE, lambda e, lt=lt, bk=bk: e.transpose(
                            pall[:, bk, 0:128], s32[:, lt * 128:(lt + 1) * 128],
                            ident[:]))
                        waitl("ve", [ttp])
                        tev = opi("ve", cVE, lambda e, lt=lt, rt=rt, bk=bk: e.tensor_copy(
                            slotAB[:, lt, rt * 128:(rt + 1) * 128], pall[:, bk, 0:128]))
                        tok_tb01[bk] = tev
                    tok_s32 = ttp
                tok_sT = (cVE, cVE.n)
                ckpt('M9.%d' % c)

                tok_fbank = [tok_tb01[0], tok_tb01[1], None, None]
                for kt in range(NK):
                    par = kt % 2
                    waitl("sc", [tok_tabFb[par]])
                    tld = opi("sc", cTabF[par], lambda e, kt=kt, par=par:
                              e.dma_start(out=tabFbuf[:, par, 0:NTT, :],
                                          in_=tabFS[kt]), 16)
                    bk = kt % 4
                    waitl("te", [tld, tok_sT, tok_fbank[kt % 4]])
                    for tt in range(NTT):
                        if tt < NTT - 1:
                            op("te", lambda e, par=par, tt=tt, bk=bk: e.matmul(
                                pall[:, bk, 0:CH], tabFbuf[:, par, tt, :],
                                slotAB[:, tt, :], start=(tt == 0), stop=False))
                        else:
                            tmm = opi("te", cTE, lambda e, par=par, tt=tt, bk=bk:
                                      e.matmul(pall[:, bk, 0:CH],
                                               tabFbuf[:, par, tt, :],
                                               slotAB[:, tt, :],
                                               start=False, stop=True))
                    tok_tabFb[par] = tmm
                    waitl("ve", [tmm, tok_fsp[0], tok_fsp[1], (cSC, cSC.n)])
                    tev = opi("ve", cVE, lambda e, kt=kt, bk=bk:
                              e.tensor_copy(fslot[:, 0, kt, :], pall[:, bk, 0:CH]))
                    tok_fbank[kt % 4] = tev
                tok_S = (cVE, cVE.n)

                tok_ylast = None
                tok_Ykt = [None] * NK
                for kt in range(NK):
                    waitl("sy", [tok_frl, tok_fsp[0], tok_fsp[1]])
                    tld = opi("sy", cFrl, lambda e, kt=kt: e.dma_start(
                        out=freload[:],
                        in_=fscr[:, :, kt].rearrange("h p r -> p h r")), 16)
                    waitl("ve", [tld, tok_S, tok_ylast])
                    opi("ve", cVE, lambda e, kt=kt: e.tensor_mul(
                        fslot[:, 1, kt, :], freload[:, 0, :], fslot[:, 0, kt, :]))
                    tgp = opi("ve", cVE, lambda e, kt=kt: e.scalar_tensor_tensor(
                        fslot[:, 0, kt, :], freload[:, 1, :], -1.0,
                        fslot[:, 0, kt, :], op0=ALU.mult, op1=ALU.mult))
                    tok_ylast = tgp
                    tok_frl = tgp
                    tok_Ykt[kt] = tgp
                tok_Y = (cVE, cVE.n)

                for rt in range(NRT):
                    opi("ve", cVE, lambda e, rt=rt: e.memset(sv[:, 24 + rt:25 + rt], 0.0))
                    tzero = opi("ve", cVE,
                                lambda e, rt=rt: e.memset(sv[:, 28 + rt:29 + rt], 0.0))
                for lc in range(NHC):
                    par2 = lc % 2
                    if par2 == 0:
                        waitl("gp", [tok_tabI2b[0], tok_tabIb[0], tok_tabIb[1],
                                     (cTabI[0], cTabI[0].n), (cTabI[1], cTabI[1].n)])
                        tld = opi("gp", cTabI2[0], lambda e, lc=lc: e.dma_start(
                            out=tabI2buf[:], in_=tabI2[lc]), 16)
                    else:
                        waitl("gp", [tok_tabI2b[1], (cTE, cTE.n)])
                        tld = opi("gp", cTabI2[1], lambda e, lc=lc: e.dma_start(
                            out=tabI2bA[:], in_=tabI2[lc, :, 0:NKA]), 16)
                        if NKB > 0:
                            tld = opi("gp", cTabI2[1], lambda e, lc=lc: e.dma_start(
                                out=tabI2bB[:], in_=tabI2[lc, :, NKA:]), 16)
                    txl = []
                    for rt in range(NRT):
                        waitl("sy", [tok_xlc[rt]])
                        t = opi("sy", cXlc[rt], lambda e, rb=rb, rt=rt, lc=lc: e.dma_start(
                            out=xlc[:, rt, :],
                            in_=x_in[rb + rt * 128: rb + (rt + 1) * 128,
                                     lc * TC:(lc + 1) * TC]), 16)
                        txl.append(t)
                    for rt in range(NRT):
                        bk = 4 + rt
                        waitl("te", [tld, tok_cbank[rt]])
                        for kt in range(NK):
                            if par2 == 0:
                                rh0 = tabI2buf[:, kt, 0, :]
                                rh1 = tabI2buf[:, kt, 1, :]
                            elif kt < NKA:
                                rh0 = tabI2bA[:, kt, 0, :]
                                rh1 = tabI2bA[:, kt, 1, :]
                            else:
                                rh0 = tabI2bB[:, kt - NKA, 0, :]
                                rh1 = tabI2bB[:, kt - NKA, 1, :]
                            waitl("te", [tok_Ykt[kt]])
                            op("te", lambda e, kt=kt, rt=rt, bk=bk, rh0=rh0: e.matmul(
                                pall[:, bk, :], fslot[:, 1, kt, rt * 128:(rt + 1) * 128],
                                rh0, start=(kt == 0), stop=False))
                            if kt < NK - 1:
                                op("te", lambda e, kt=kt, rt=rt, bk=bk, rh1=rh1: e.matmul(
                                    pall[:, bk, :],
                                    fslot[:, 0, kt, rt * 128:(rt + 1) * 128],
                                    rh1, start=False, stop=False))
                            else:
                                tmm = opi("te", cTE,
                                          lambda e, kt=kt, rt=rt, bk=bk, rh1=rh1:
                                          e.matmul(pall[:, bk, :],
                                                   fslot[:, 0, kt, rt * 128:(rt + 1) * 128],
                                                   rh1, start=False, stop=True))
                        zp = (lc * NRT + rt) % 2
                        waitl("ve", [tmm, txl[rt], tok_zst[zp], tzero])
                        tz = opi("ve", cVE, lambda e, zp=zp, bk=bk, rt=rt: e.tensor_add(
                            zstage[:, zp, :], pall[:, bk, :], xlc[:, rt, :]))
                        tok_cbank[rt] = tz
                        tok_xlc[rt] = tz
                        waitl("ve", [tz])
                        tr1 = opi("ve", cVE, lambda e, zp=zp: e.reduce_sum(
                            col(22), zstage[:, zp, :], axis=AX.X))
                        waitl("ve", [tr1])
                        opi("ve", cVE, lambda e, rt=rt: e.tensor_add(
                            sv[:, 24 + rt:25 + rt], sv[:, 24 + rt:25 + rt], col(22)))
                        waitl("sc", [tz, tok_sq16])
                        tsq = opi("sc", cSC, lambda e, zp=zp: e.activation(
                            sq16[:], zstage[:, zp, :], AF.Square))
                        waitl("ve", [tsq])
                        tr2 = opi("ve", cVE, lambda e: e.reduce_sum(
                            col(23), sq16[:], axis=AX.X))
                        tok_sq16 = tr2
                        waitl("ve", [tr2])
                        opi("ve", cVE, lambda e, rt=rt: e.tensor_add(
                            sv[:, 28 + rt:29 + rt], sv[:, 28 + rt:29 + rt], col(23)))
                        waitl("sy", [tz])
                        tst = opi("sy", cZst[zp], lambda e, rb=rb, rt=rt, lc=lc, zp=zp:
                                  e.dma_start(
                                      out=x1scr[rb + rt * 128: rb + (rt + 1) * 128,
                                                lc * TC:(lc + 1) * TC],
                                      in_=zstage[:, zp, :]), 16)
                        tok_zst[zp] = tst
                    tok_tabI2b[par2] = tmm
                for rt in range(NRT):
                    gi = c * NRT + rt
                    tmn = opi("ve", cVE, lambda e, rt=rt: e.tensor_scalar(
                        col(34), sv[:, 24 + rt:25 + rt], 1.0 / D, None, op0=ALU.mult))
                    t2n = opi("ve", cVE, lambda e, rt=rt: e.tensor_scalar(
                        col(35), sv[:, 28 + rt:29 + rt], 1.0 / D, None, op0=ALU.mult))
                    waitl("ve", [tmn, t2n])
                    tvv = opi("ve", cVE, lambda e: e.tensor_mul(col(36), col(34), col(34)))
                    waitl("ve", [tvv])
                    tvs = opi("ve", cVE, lambda e: e.tensor_sub(col(37), col(35), col(36)))
                    waitl("ve", [tvs])
                    tva = opi("ve", cVE, lambda e: e.tensor_add(col(37), col(37), col(1)))
                    waitl("sc", [tva])
                    tsd = opi("sc", cSC, lambda e: e.activation(col(36), col(37), AF.Sqrt))
                    waitl("ve", [tsd])
                    trs = opi("ve", cVE, lambda e, gi=gi: e.reciprocal(
                        stats[:, gi, 0:1], col(36)))
                    waitl("ve", [trs])
                    tmr = opi("ve", cVE, lambda e, gi=gi: e.tensor_mul(
                        col(36), col(34), stats[:, gi, 0:1]))
                    waitl("ve", [tmr])
                    opi("ve", cVE, lambda e, gi=gi: e.tensor_scalar(
                        stats[:, gi, 1:2], col(36), -1.0, None, op0=ALU.mult))

            ckpt('M')
            tok_stats = (cVE, cVE.n)
            tok_z0 = (cZst[0], cZst[0].n)
            tok_z1 = (cZst[1], cZst[1].n)

            barrier()
            t_lr = opi("sy", cLnw, lambda e: e.dma_start(out=lnw_row[:], in_=lnw_in), 16)
            tok_bbank = [None, None]
            for hcx in range(NHC):
                bb = hcx % 2
                waitl("te", [t_or, t_lr, tok_stats, tok_bbank[bb]])
                tmm = opi("te", cTE, lambda e, hcx=hcx, bb=bb: e.matmul(
                    pall[:, bb, :], ones_row[:, :],
                    lnw_row[:, hcx * TC:(hcx + 1) * TC], start=True, stop=True))
                waitl("ve", [tmm])
                tok_bbank[bb] = opi("ve", cVE, lambda e, hcx=hcx, bb=bb: e.tensor_copy(
                    lnwb[:, hcx * TC:(hcx + 1) * TC], pall[:, bb, :]))
            tok_lnwb = (cVE, cVE.n)
            ckpt('Nb')

            tok_x1 = None
            tok_nxs = None
            tok_tb01 = [tok_lnwb, None]
            for gi in range(NTILE):
                waitl("sy", [tok_x1, tok_z0, tok_z1])
                tld = opi("sy", cX1ld, lambda e, gi=gi: e.dma_start(
                    out=x1stage[:], in_=x1scr[gi * 128:(gi + 1) * 128, :]), 16)
                waitl("ve", [tld, tok_stats, tok_nxs, tok_lnwb])
                tx1 = opi("ve", cVE, lambda e, gi=gi: e.tensor_scalar(
                    nxstage[:], x1stage[:], stats[:, gi, 0:1], stats[:, gi, 1:2],
                    op0=ALU.mult, op1=ALU.add))
                tok_x1 = tx1
                waitl("sc", [tx1])
                tsq = opi("sc", cSC, lambda e: e.activation(nsq16[:], nxstage[:], AF.Square))
                waitl("ve", [tsq])
                tr2 = opi("ve", cVE, lambda e: e.reduce_sum(col(23), nsq16[:], axis=AX.X))
                waitl("ve", [tr2])
                tms = opi("ve", cVE, lambda e: e.tensor_scalar(
                    col(27), col(23), 1.0 / D, col(1), op0=ALU.mult, op1=ALU.add))
                waitl("sc", [tms])
                tsd = opi("sc", cSC, lambda e: e.activation(col(26), col(27), AF.Sqrt))
                waitl("ve", [tsd])
                trs = opi("ve", cVE, lambda e: e.reciprocal(col(25), col(26)))
                waitl("ve", [trs])
                opi("ve", cVE, lambda e: e.tensor_scalar(
                    nxstage[:], nxstage[:], col(25), None, op0=ALU.mult))
                tnl = opi("ve", cVE, lambda e: e.tensor_mul(
                    nxstage[:], nxstage[:], lnwb[:]))
                for lt in range(NL):
                    bk = lt % 2
                    waitl("te", [tnl, tok_tb01[bk]])
                    ttp = opi("te", cTE, lambda e, lt=lt, bk=bk: e.transpose(
                        pall[:, bk, 0:128], nxstage[:, lt * 128:(lt + 1) * 128], ident[:]))
                    waitl("ve", [ttp])
                    tev = opi("ve", cVE, lambda e, lt=lt, gi=gi, bk=bk: e.tensor_copy(
                        nx16[:, lt, gi * 128:(gi + 1) * 128], pall[:, bk, 0:128]))
                    tok_tb01[bk] = tev
                tok_nxs = ttp
                ckpt('N.%d' % gi)
            tok_nx = (cVE, cVE.n)
            ckpt('N')

            barrier()
            tok_gb = [None, None]
            tok_ub = [None, None]
            tok_dsin = [None, None]
            tok_dsout = [None, None]
            tok_hst = [None, None]
            tok_gu_banks = [None] * 4
            tok_sig = [None, None]
            tok_h1 = [None, None]
            for ic in range(NI):
                par = ic % 2
                waitl("gp", [tok_gb[par]])
                tgl = opi("gp", cGb[par], lambda e, ic=ic, par=par: e.dma_start(
                    out=gbuf[:, par], in_=gateT[ic]), 16)
                waitl("gp", [tok_ub[par]])
                tul = opi("gp", cUb[par], lambda e, ic=ic, par=par: e.dma_start(
                    out=ubuf[:, par], in_=upT[ic]), 16)
                waitl("gp", [tok_dsout[par]])
                tdl = opi("gp", cDstg[par], lambda e, ic=ic, par=par: e.dma_start(
                    out=dstage[:, par, :], in_=downT[ic * 128:(ic + 1) * 128, :]), 16)
                waitl("sc", [tdl])
                tds = opi("sc", cDscr[par], lambda e, ic=ic, par=par: e.dma_start(
                    out=dscr[ic], in_=dstage[:, par, :]), 16)
                tok_dsout[par] = tds

                last_h = None
                for rc in range(NRC):
                    bkg = (rc % 2) * 2
                    bku = bkg + 1
                    waitl("te", [tgl, tul, tok_nx, tok_gu_banks[bkg], tok_gu_banks[bku]])
                    for lt in range(NL):
                        op("te", lambda e, lt=lt, par=par, rc=rc, bkg=bkg: e.matmul(
                            pall[:, bkg, 0:ATC], gbuf[:, par, lt, :],
                            nx16[:, lt, rc * ATC:(rc + 1) * ATC],
                            start=(lt == 0), stop=(lt == NL - 1)))
                    for lt in range(NL):
                        if lt < NL - 1:
                            op("te", lambda e, lt=lt, par=par, rc=rc, bku=bku: e.matmul(
                                pall[:, bku, 0:ATC], ubuf[:, par, lt, :],
                                nx16[:, lt, rc * ATC:(rc + 1) * ATC],
                                start=(lt == 0), stop=False))
                        else:
                            tmm = opi("te", cTE,
                                      lambda e, lt=lt, par=par, rc=rc, bku=bku:
                                      e.matmul(pall[:, bku, 0:ATC], ubuf[:, par, lt, :],
                                               nx16[:, lt, rc * ATC:(rc + 1) * ATC],
                                               start=False, stop=True))
                    if rc == NRC - 1:
                        tok_gb[par] = tmm
                        tok_ub[par] = tmm
                    sp = rc % 2
                    waitl("sc", [tmm, tok_sig[sp]])
                    tsg = opi("sc", cSC, lambda e, bkg=bkg, sp=sp: e.activation(
                        asig[:, sp, 0:ATC], pall[:, bkg, 0:ATC], AF.Sigmoid))
                    waitl("ve", [tsg, tok_h1[sp], tok_hst[par] if rc == 0 else None])
                    th1 = opi("ve", cVE, lambda e, bkg=bkg, sp=sp: e.tensor_mul(
                        ah1[:, sp, 0:ATC], asig[:, sp, 0:ATC], pall[:, bkg, 0:ATC]))
                    tok_sig[sp] = th1
                    waitl("ve", [th1])
                    th2 = opi("ve", cVE, lambda e, bku=bku, sp=sp, rc=rc, par=par:
                              e.tensor_mul(hstage[:, par, rc, :], ah1[:, sp, 0:ATC],
                                           pall[:, bku, 0:ATC]))
                    tok_h1[sp] = th2
                    tok_gu_banks[bkg] = th2
                    tok_gu_banks[bku] = th2
                    last_h = th2
                waitl("sy", [last_h])
                tho = opi("sy", cHsto[par], lambda e, ic=ic, par=par: e.dma_start(
                    out=hscr.rearrange("g p i r -> i p g r")[ic],
                    in_=hstage[:, par]), 16)
                tok_hst[par] = tho
            tok_A_h0 = (cHsto[0], cHsto[0].n)
            tok_A_h1 = (cHsto[1], cHsto[1].n)
            tok_A_te = (cTE, cTE.n)
            ckpt('A')

            barrier()
            HQ = max(1, D // 1024)
            HQW = D // HQ
            NH2 = HQW // TC
            tok_hslab = tok_A_te
            tok_dbuf = [None] * 6
            tok_bank8 = [None] * 8
            tok_x1q = [None] * 4
            tok_zq = [None, None]
            tok_sq16q = None
            tok_bz = None
            tok_outd = None
            for rg in range(NRG):
                waitl("sy", [tok_hslab, tok_A_h0, tok_A_h1])
                thl = opi("sy", cHsl, lambda e, rg=rg: e.dma_start(
                    out=hslab[:], in_=hscr[rg]), 16)
                tzro = None
                for q in range(8):
                    tzro = opi("ve", cVE, lambda e, q=q: e.memset(
                        sv[:, 24 + q:25 + q], 0.0))
                for hcq in range(HQ):
                    tok_txq = [None] * NGRT
                    for grt in range(NGRT):
                        gi = rg * NGRT + grt
                        xp = grt % 4
                        waitl("sy", [tok_x1q[xp], tok_z0, tok_z1])
                        tok_txq[grt] = opi("sy", cBx1[xp],
                                           lambda e, gi=gi, hcq=hcq, xp=xp: e.dma_start(
                                               out=x1q[:, xp, :],
                                               in_=x1scr[gi * 128:(gi + 1) * 128,
                                                         hcq * HQW:(hcq + 1) * HQW]), 16)
                    for ic in range(NI):
                        dp = ic % 6
                        qe = ("sy", "sc", "gp")[ic % 3]
                        waitl(qe, [tok_dbuf[dp], tok_dsout[0], tok_dsout[1]])
                        tdl = opi(qe, cDbuf[dp], lambda e, ic=ic, dp=dp, hcq=hcq:
                                  e.dma_start(out=dbufq[:, dp, :],
                                              in_=dscr[ic][:, hcq * HQW:(hcq + 1) * HQW]),
                                  16)
                        waitl("te", [tdl, thl])
                        for grt in range(NGRT):
                            for h2 in range(NH2):
                                bk = grt * NH2 + h2
                                if ic == 0:
                                    waitl("te", [tok_bank8[bk]])
                                if grt == NGRT - 1 and h2 == NH2 - 1:
                                    tmm = opi("te", cTE,
                                              lambda e, ic=ic, grt=grt, h2=h2, bk=bk, dp=dp:
                                              e.matmul(pall[:, bk, :],
                                                       hslab[:, ic, grt * 128:(grt + 1) * 128],
                                                       dbufq[:, dp, h2 * TC:(h2 + 1) * TC],
                                                       start=(ic == 0), stop=(ic == NI - 1)))
                                else:
                                    op("te", lambda e, ic=ic, grt=grt, h2=h2, bk=bk, dp=dp:
                                       e.matmul(pall[:, bk, :],
                                                hslab[:, ic, grt * 128:(grt + 1) * 128],
                                                dbufq[:, dp, h2 * TC:(h2 + 1) * TC],
                                                start=(ic == 0), stop=(ic == NI - 1)))
                        tok_dbuf[dp] = tmm
                    for grt in range(NGRT):
                        gi = rg * NGRT + grt
                        xp = grt % 4
                        zp = grt % 2
                        waitl("ve", [tok_txq[grt], tok_stats, tok_zq[zp], tzro])
                        tx1 = opi("ve", cVE, lambda e, gi=gi, xp=xp: e.tensor_scalar(
                            x1q[:, xp, :], x1q[:, xp, :], stats[:, gi, 0:1],
                            stats[:, gi, 1:2], op0=ALU.mult, op1=ALU.add))
                        for h2 in range(NH2):
                            bk = grt * NH2 + h2
                            waitl("ve", [tmm])
                            tz = opi("ve", cVE, lambda e, bk=bk, h2=h2, xp=xp, zp=zp:
                                     e.tensor_add(zq[:, zp, h2 * TC:(h2 + 1) * TC],
                                                  pall[:, bk, :],
                                                  x1q[:, xp, h2 * TC:(h2 + 1) * TC]))
                            tok_bank8[bk] = tz
                        tok_x1q[xp] = tz
                        tr1 = opi("ve", cVE, lambda e, zp=zp: e.reduce_sum(
                            col(22), zq[:, zp, :], axis=AX.X))
                        ta1 = opi("ve", cVE, lambda e, grt=grt: e.tensor_add(
                            sv[:, 24 + grt:25 + grt], sv[:, 24 + grt:25 + grt], col(22)))
                        waitl("sc", [tz, tok_sq16q])
                        tsq = opi("sc", cSC, lambda e, zp=zp: e.activation(
                            sq16q[:], zq[:, zp, :], AF.Square))
                        waitl("ve", [tsq])
                        tr2 = opi("ve", cVE, lambda e: e.reduce_sum(
                            col(23), sq16q[:], axis=AX.X))
                        tok_sq16q = tr2
                        ta2 = opi("ve", cVE, lambda e, grt=grt: e.tensor_add(
                            sv[:, 28 + grt:29 + grt], sv[:, 28 + grt:29 + grt], col(23)))
                        waitl("sy", [ta1, tr2])
                        tzs = opi("sy", cZq[zp], lambda e, gi=gi, hcq=hcq, zp=zp:
                                  e.dma_start(out=x1scr[gi * 128:(gi + 1) * 128,
                                                        hcq * HQW:(hcq + 1) * HQW],
                                              in_=zq[:, zp, :]), 16)
                        tok_zq[zp] = tzs
                tok_z0 = (cZq[0], cZq[0].n)
                tok_z1 = (cZq[1], cZq[1].n)
                for grt in range(NGRT):
                    gi = rg * NGRT + grt
                    tmn = opi("ve", cVE, lambda e, grt=grt: e.tensor_scalar(
                        col(34), sv[:, 24 + grt:25 + grt], 1.0 / D, None, op0=ALU.mult))
                    t2n = opi("ve", cVE, lambda e, grt=grt: e.tensor_scalar(
                        col(35), sv[:, 28 + grt:29 + grt], 1.0 / D, None, op0=ALU.mult))
                    tvv = opi("ve", cVE, lambda e: e.tensor_mul(col(36), col(34), col(34)))
                    tvs = opi("ve", cVE, lambda e: e.tensor_sub(col(37), col(35), col(36)))
                    tva = opi("ve", cVE, lambda e: e.tensor_add(col(37), col(37), col(1)))
                    waitl("sc", [tva])
                    tsd = opi("sc", cSC, lambda e: e.activation(col(36), col(37), AF.Sqrt))
                    waitl("ve", [tsd])
                    trs = opi("ve", cVE, lambda e, gi=gi: e.reciprocal(
                        statsB[:, gi, 0:1], col(36)))
                    tmr = opi("ve", cVE, lambda e, gi=gi: e.tensor_mul(
                        col(36), col(34), statsB[:, gi, 0:1]))
                    opi("ve", cVE, lambda e, gi=gi: e.tensor_scalar(
                        statsB[:, gi, 1:2], col(36), -1.0, None, op0=ALU.mult))
                tok_statsB = (cVE, cVE.n)
                for grt in range(NGRT):
                    gi = rg * NGRT + grt
                    waitl("sy", [tok_bz, tok_z0, tok_z1])
                    tbl = opi("sy", cX1ld, lambda e, gi=gi: e.dma_start(
                        out=bz[:], in_=x1scr[gi * 128:(gi + 1) * 128, :]), 16)
                    waitl("ve", [tbl, tok_statsB, tok_outd])
                    tfin = opi("ve", cVE, lambda e, gi=gi: e.tensor_scalar(
                        bz[:], bz[:], statsB[:, gi, 0:1], statsB[:, gi, 1:2],
                        op0=ALU.mult, op1=ALU.add))
                    waitl("sy", [tfin])
                    tou = opi("sy", cOut, lambda e, gi=gi: e.dma_start(
                        out=out_d[gi * 128:(gi + 1) * 128, :], in_=bz[:]), 16)
                    tok_outd = tou
                    tok_bz = tou
                tok_hslab = (cTE, cTE.n)

            waitl("sy", [(cOut, cOut.n)])
            op("sy", lambda e: e.nop())

        try:
            _emit_all()
        except _Stop:
            for ek in Emit.ENGINES:
                for ct in ALLCT:
                    em.wait(ek, (ct, ct.n))
            tdmy = opi("sy", cOut, lambda e: e.dma_start(
                out=out_d[0:128, 0:128], in_=ident[:]), 16)
            em.wait("sy", tdmy)


        block = E(nc.Block())

        @block.sync
        def _(eng):
            em.replay("sy", eng)

        @block.scalar
        def _(eng):
            em.replay("sc", eng)

        @block.vector
        def _(eng):
            em.replay("ve", eng)

        @block.tensor
        def _(eng):
            em.replay("te", eng)

        @block.gpsimd
        def _(eng):
            em.replay("gp", eng)

    return nc


def make_tables_host(D, KP):
    li = np.arange(D, dtype=np.int64)
    ki = np.arange(KP, dtype=np.int64)
    mf = np.mod(np.outer(li, ki), D)
    mi = np.mod(np.outer(ki, li), D)
    midx_f_flat = np.stack([(mf + 3 * D // 4) % D, (mf + D // 2) % D]).astype(np.uint16)
    NL = D // 128
    NKl = KP // 128
    midx_f = np.ascontiguousarray(
        midx_f_flat.reshape(2, NL, 128, NKl, 128).transpose(0, 3, 2, 1, 4))
    midx_i = np.stack([(mi + 3 * D // 4) % D, (mi + D // 2) % D]).astype(np.uint16)
    alpha = np.where((ki == 0) | (ki == D // 2), 1.0, 2.0)
    alpha[ki > D // 2] = 0.0
    alpha = (alpha / D).astype(np.float64)
    NK = KP // 128
    a_l = alpha.reshape(NK, 128).T
    alpha_in = np.stack([a_l, -a_l]).astype(np.float32)
    return midx_f, midx_i, alpha_in


def kernel(L0_attn_x, L0_ffn_ln_w, L0_ffn_gate_w, L0_ffn_up_w, L0_ffn_down_w):
    x = np.ascontiguousarray(np.asarray(L0_attn_x, dtype=np.float32))
    B, S, D = x.shape
    I = L0_ffn_gate_w.shape[0]
    RC = (B * S) // N_CORES
    SL = S // N_CORES

    NL_, NI_ = D // 128, I // 128
    gateT = np.ascontiguousarray(np.asarray(L0_ffn_gate_w, np.float32).T.reshape(
        NL_, 128, NI_, 128).transpose(2, 1, 0, 3))
    upT = np.ascontiguousarray(np.asarray(L0_ffn_up_w, np.float32).T.reshape(
        NL_, 128, NI_, 128).transpose(2, 1, 0, 3))
    downT = np.ascontiguousarray(np.asarray(L0_ffn_down_w, np.float32).T)
    lnw = np.ascontiguousarray(np.asarray(L0_ffn_ln_w, np.float32).reshape(1, D))
    KP = ((D // 2 + 1 + 127) // 128) * 128
    midx_f, midx_i, alpha_in = make_tables_host(D, KP)
    ident = np.eye(128, dtype=np.float32)

    in_maps = []
    for c in range(N_CORES):
        xs = np.ascontiguousarray(x[:, c * SL:(c + 1) * SL, :].reshape(RC, D))
        in_maps.append(dict(
            x_in=xs, gateT=gateT, upT=upT, downT=downT, lnw_in=lnw,
            midx_f=midx_f, midx_i=midx_i, alpha_in=alpha_in, ident_in=ident))

    nc = build_program(D=D, I=I, RC=RC, CH=512)
    res = run_bass_kernel_spmd(nc, in_maps, core_ids=list(range(N_CORES)))

    out = np.empty((B, S, D), dtype=np.float32)
    for c in range(N_CORES):
        out[:, c * SL:(c + 1) * SL, :] = res.results[c]["out_d"].reshape(B, SL, D)
    return out
